# version 04
# baseline (speedup 1.0000x reference)
import numpy as np
import ml_dtypes
import concourse.bass as bass
import concourse.mybir as mybir
from concourse.bass_utils import run_bass_kernel_spmd

F32 = mybir.dt.float32
BF16 = mybir.dt.bfloat16
AF = mybir.ActivationFunctionType
ALU = mybir.AluOpType
AX = mybir.AxisListType.X
NBF = np.dtype(ml_dtypes.bfloat16)

T, D = 8192, 1024
NSEG, SEGW, SEG = 16, 512, 768
LW = NSEG * SEG
OWN = 2048
ENGS = ("sync", "gpsimd", "vector", "scalar", "tensor")
L0PERM = [h for o in range(8) for h in (o, o + 8)]
PERM_COLS = np.concatenate([np.arange(h * 64, (h + 1) * 64) for h in L0PERM])

STAGE_LO, STAGE_HI = 0, 9
STAGES = ["ln0", "ffn0", "att0", "cmb0", "ln1", "qkv1", "att1", "proj1",
          "moe", "fin"]


def block_order(p):
    own = [4 * c + p for c in range(4)]
    rest = [g for g in range(16) if g % 4 != p]
    return own + rest


def kb_blocks(c):
    out = []
    for lb in range(c + 1):
        out.append((lb, lb == c))
    for j in range(3 * (c + 1)):
        out.append((4 + j, j >= 3 * c))
    return out


class Prog:
    def __init__(self):
        self.nc = bass.Bass("TRN2", target_bir_lowering=False)
        self.ops = []
        self._keep = []

    def din(self, name, shape, dtype=F32):
        return self.nc.dram_tensor(name, shape, dtype, kind="ExternalInput").ap()

    def dhand(self, name, shape, dtype, producer):
        kind = ("ExternalInput" if STAGES.index(producer) < STAGE_LO
                else "ExternalOutput")
        return self.nc.dram_tensor(name, shape, dtype, kind=kind).ap()

    def sb(self, name, shape, dtype):
        return self.nc.alloc_sbuf_tensor(name, shape, dtype)

    def psum(self, name, shape, dtype=F32):
        cm = self.nc.psum_tensor(name, shape, dtype)
        t = cm.__enter__()
        self._keep.append(cm)
        return t

    def _e(self, eng, meth, args, kw):
        self.ops.append(
            (eng, lambda e, m=meth, a=args, k=kw: getattr(e, m)(*a, **k),
             1, False))

    def v(self, meth, *args, **kw):
        self._e("vector", meth, args, kw)

    def s(self, meth, *args, **kw):
        self._e("scalar", meth, args, kw)

    def t(self, meth, *args, **kw):
        self._e("tensor", meth, args, kw)

    def dma(self, out, in_, q="sync"):
        self.ops.append(
            (q, lambda e, o=out, i=in_: e.dma_start(out=o, in_=i), 16, False))

    def dma_group(self, pairs, q="sync"):
        def fn(e, g, pairs=tuple(pairs)):
            for (o, i_) in pairs:
                e.dma_start(out=o, in_=i_).then_inc(g, 16)
        self.ops.append((q, fn, 16 * len(pairs), True))

    def vgroup(self, items):
        def fn(e, items=tuple(items)):
            for (m, a, k) in items:
                ins = getattr(e, m)(*a, **k)
            return ins
        self.ops.append(("vector", fn, 1, False))

    def mm(self, items):
        def fn(e, items=tuple(items)):
            for (o, l, r, st, sp) in items:
                ins = e.matmul(o, l, r, start=st, stop=sp)
            return ins
        self.ops.append(("tensor", fn, 1, False))

    def finish(self):
        cm = self.nc.semaphore("g")
        g = cm.__enter__()
        self._keep.append(cm)
        sched = []
        tot = 0
        for eng, fn, inc, selfinc in self.ops:
            sched.append((eng, fn, inc, selfinc, tot))
            tot += inc
        with self.nc.Block() as blk:
            for engname in ENGS:
                @getattr(blk, engname)
                def _(e, engname=engname):
                    for eng, fn, inc, selfinc, before in sched:
                        if eng != engname:
                            continue
                        if before:
                            e.wait_ge(g, before)
                        if selfinc:
                            fn(e, g)
                        else:
                            fn(e).then_inc(g, inc)
                    e.wait_ge(g, tot)
        return self


def build():
    p = Prog()
    lo, hi = STAGE_LO, STAGE_HI

    def on(st):
        return lo <= STAGES.index(st) <= hi

    xTp = p.din("xTp", [D, LW])
    wqp_d = p.din("wqp", [D, D])
    bqp_d = p.din("bqp", [128, 8])
    wk_d = p.din("wk0", [D, 128])
    bk_d = p.din("bk0", [128, 1])
    wv_d = p.din("wv0", [D, 128])
    bv_d = p.din("bv0", [1, 128])
    band_d = p.din("band", [128, 384], BF16)
    km_d = p.din("km", [128, LW], BF16)
    vm_d = p.din("vm", [128, NSEG * 6])
    wo_d = p.din("wo", [D, D])
    bo_d = p.din("bo", [128, 8])
    wrT_d = p.din("wrT", [D, D])
    wkT_d = p.din("wkT", [D, D])
    wvT_d = p.din("wvT", [D, D])
    mur_d = p.din("mur", [128, 8])
    muk_d = p.din("muk", [128, 8])
    wq1_d = p.din("wq1", [D, D])
    wk1_d = p.din("wk1", [D, D])
    wv1_d = p.din("wv1", [D, D])
    wr_d = p.din("wr", [128, 64])
    br_d = p.din("br", [1, 8])
    wp_d = p.din("wp", [D, D])
    bp_d = p.din("bp", [128, 8])
    w1_d = p.din("w1", [8, D, 4096])
    w2_d = p.din("w2", [8, 4096, D])
    gam_d = p.din("gam", [128, 8])
    bet_d = p.din("bet", [128, 8])
    ident_d = p.din("ident", [128, 128])
    pmsk_d = p.din("pmsk", [128, 64, 512], BF16)
    xn0d = p.dhand("xn0d", [D, LW], BF16, "ln0")
    g0d = p.dhand("g0d", [D, T], BF16, "ffn0")
    a0d = p.dhand("a0d", [D, T], BF16, "att0")
    x1d = p.dhand("x1d", [D, T], F32, "cmb0")
    xn1d = p.dhand("xn1d", [D, T], F32, "ln1")
    xn1bd = p.dhand("xn1bd", [D, T], BF16, "ln1")
    k1d = p.dhand("k1d", [D, T], BF16, "qkv1")
    vbd = p.dhand("vbd", [128, 64, D], BF16, "qkv1")
    qTd = p.dhand("qTd", [D, OWN], BF16, "qkv1")
    mskTd = p.dhand("mskTd", [16, OWN], F32, "qkv1")
    a1d = p.dhand("a1d", [D, OWN], BF16, "att1")
    y1d = p.dhand("y1d", [D, OWN], F32, "proj1")
    yTd = p.dhand("yTd", [D, OWN], F32, "moe")
    outd = p.nc.dram_tensor("outd", [D, OWN], F32, kind="ExternalOutput").ap()

    BA = p.sb("BA", [128, 16384], BF16)
    BB = p.sb("BB", [128, 16384], BF16)
    BC = p.sb("BC", [128, 16384], BF16)
    BD = p.sb("BD", [128, 8192], BF16)
    F1 = p.sb("F1", [128, 8192], F32)
    F2 = p.sb("F2", [128, 4096], F32)
    SM = p.sb("SM", [128, 2048], F32)
    RW = p.sb("RW", [16, 2048], F32)
    R1 = p.sb("R1", [1, 1024], F32)
    ST = p.sb("ST", [1, 2048], F32)
    SB2 = p.sb("SB2", [128, 2048], BF16)
    ID = p.sb("ID", [128, 128], F32)
    ONES = p.sb("ONES", [128, 1], F32)
    ONESR = p.sb("ONESR", [1, 128], F32)
    pa = p.psum("pa", [128, 512], F32)
    pv = p.psum("pv", [128, 512], F32)
    sps = p.psum("sps", [128, 512], F32)
    ops_ = p.psum("ops", [128, 260], F32)
    pt = p.psum("pt", [128, 128], F32)
    pr = p.psum("pr", [128, 128], F32)
    prow = p.psum("prow", [1, 512], F32)

    p.v("memset", ONES[:], 1.0)
    p.v("memset", ONESR[:], 1.0)
    p.v("memset", ST[0:1, 2040:2041], 1e-5)
    p.dma(ID[:], ident_d[:, :])

    def cast_w(dst, src, cols=1024):
        p.dma_group([(F1[:, i * cols:(i + 1) * cols],
                      src[i * 128:(i + 1) * 128, 0:cols]) for i in range(8)])
        p.vgroup([("tensor_copy",
                   (dst[:, i * cols:(i + 1) * cols],
                    F1[:, i * cols:(i + 1) * cols]), {}) for i in range(8)])

    def fm_ln(src_ap, width, out_tiles):
        assert width % 512 == 0
        w = 512
        for t0 in range(0, width, 512):
            p.dma_group([(F1[:, i * w:(i + 1) * w],
                          src_ap[i * 128:(i + 1) * 128, t0:t0 + w])
                         for i in range(8)])
            p.mm([(prow[0:1, 0:w], ONES[:], F1[:, i * w:(i + 1) * w],
                   i == 0, i == 7) for i in range(8)])
            p.v("tensor_scalar_mul", ST[0:1, 0:w], prow[0:1, 0:w], 1.0 / D)
            for i in range(8):
                p.s("activation", F2[:, 0:w], F1[:, i * w:(i + 1) * w],
                    AF.Square)
                p.mm([(prow[0:1, 0:w], ONES[:], F2[:, 0:w], i == 0, i == 7)])
            p.v("tensor_scalar_mul", ST[0:1, 512:512 + w], prow[0:1, 0:w],
                1.0 / D)
            p.v("tensor_mul", ST[0:1, 1024:1024 + w], ST[0:1, 0:w],
                ST[0:1, 0:w])
            p.v("tensor_tensor", ST[0:1, 512:512 + w], ST[0:1, 512:512 + w],
                ST[0:1, 1024:1024 + w], ALU.subtract)
            p.s("activation", ST[0:1, 512:512 + w], ST[0:1, 512:512 + w],
                AF.Ln, bias=ST[0:1, 2040:2041])
            p.s("activation", ST[0:1, 512:512 + w], ST[0:1, 512:512 + w],
                AF.Exp, scale=-0.5)
            p.mm([(pv[:, 0:w], ONESR[:], ST[0:1, 0:w], True, True)])
            p.v("tensor_copy", SM[:, 1024:1024 + w], pv[:, 0:w])
            p.mm([(pv[:, 0:w], ONESR[:], ST[0:1, 512:512 + w], True, True)])
            p.v("tensor_copy", SM[:, 1536:1536 + w], pv[:, 0:w])
            for i in range(8):
                p.v("tensor_tensor", F2[:, 0:w], F1[:, i * w:(i + 1) * w],
                    SM[:, 1024:1024 + w], ALU.subtract)
                p.v("tensor_tensor", F2[:, 512:512 + w], F2[:, 0:w],
                    SM[:, 1536:1536 + w], ALU.mult)
                out_tiles(i, t0, w)

    if on("ln0"):
        def ln0_out(i, t0, w):
            p.v("tensor_copy", BD[:, 0:w], F2[:, 512:512 + w])
            p.dma(xn0d[i * 128:(i + 1) * 128, t0:t0 + w], BD[:, 0:w],
                  q="gpsimd")
        fm_ln(xTp, LW, ln0_out)

    if on("ffn0"):
        mur = SM[:, 0:8]
        muk = SM[:, 8:16]
        m1r = SM[:, 16:24]
        m1k = SM[:, 24:32]
        p.dma(mur, mur_d[:, :])
        p.dma(muk, muk_d[:, :])
        p.v("tensor_scalar", m1r, mur, -1.0, 1.0, ALU.mult, ALU.add)
        p.v("tensor_scalar", m1k, muk, -1.0, 1.0, ALU.mult, ALU.add)
        cast_w(BA[:, 0:8192], wrT_d)
        cast_w(BA[:, 8192:16384], wkT_d)
        cast_w(BB[:, 0:8192], wvT_d)
        xn = BC[:, 0:4104]
        xr = BC[:, 4104:8200]
        xk = BC[:, 8200:12296]
        rf = BD[:, 512:4608]
        for sg in range(NSEG):
            base = sg * SEG
            p.dma_group([(xn[:, i * 513:(i + 1) * 513],
                          xn0d[i * 128:(i + 1) * 128, base + 127:base + 640])
                         for i in range(8)])
            own_i = [xn[:, i * 513 + 1: i * 513 + 513] for i in range(8)]
            prev_i = [xn[:, i * 513: i * 513 + 512] for i in range(8)]
            f2_i = [F2[:, i * 512:(i + 1) * 512] for i in range(8)]
            p.vgroup([("tensor_scalar_mul",
                       (f2_i[i], prev_i[i], m1r[:, i:i + 1]), {})
                      for i in range(8)])
            p.vgroup([("scalar_tensor_tensor",
                       (xr[:, i * 512:(i + 1) * 512], own_i[i],
                        mur[:, i:i + 1], f2_i[i], ALU.mult, ALU.add), {})
                      for i in range(8)])
            p.vgroup([("tensor_scalar_mul",
                       (f2_i[i], prev_i[i], m1k[:, i:i + 1]), {})
                      for i in range(8)])
            p.vgroup([("scalar_tensor_tensor",
                       (xk[:, i * 512:(i + 1) * 512], own_i[i],
                        muk[:, i:i + 1], f2_i[i], ALU.mult, ALU.add), {})
                      for i in range(8)])
            for oc in range(8):
                p.mm([(pa[:, 0:512],
                       BA[:, i * 1024 + oc * 128: i * 1024 + (oc + 1) * 128],
                       xr[:, i * 512:(i + 1) * 512], i == 0, i == 7)
                      for i in range(8)])
                p.s("activation", rf[:, oc * 512:(oc + 1) * 512], pa[:, 0:512],
                    AF.Sigmoid)
            for oc in range(8):
                p.mm([(pa[:, 0:512],
                       BA[:, 8192 + i * 1024 + oc * 128:
                          8192 + i * 1024 + (oc + 1) * 128],
                       xk[:, i * 512:(i + 1) * 512], i == 0, i == 7)
                      for i in range(8)])
                p.s("activation", F2[:, 0:512], pa[:, 0:512], AF.Relu)
                p.s("activation", xr[:, oc * 512:(oc + 1) * 512], F2[:, 0:512],
                    AF.Square)
            for oc in range(8):
                p.mm([(pa[:, 0:512],
                       BB[:, i * 1024 + oc * 128: i * 1024 + (oc + 1) * 128],
                       xr[:, i * 512:(i + 1) * 512], i == 0, i == 7)
                      for i in range(8)])
                p.v("tensor_mul", BD[:, 0:512], pa[:, 0:512],
                    rf[:, oc * 512:(oc + 1) * 512])
                p.dma(g0d[oc * 128:(oc + 1) * 128, sg * 512:(sg + 1) * 512],
                      BD[:, 0:512], q="gpsimd")

    if on("att0"):
        bqs = SM[:, 0:8]
        bks = SM[:, 8:9]
        bvb = SM[:, 128:256]
        vms = SM[:, 256:352]
        otm = SM[:, 1024:2048]
        bvr = R1[0:1, 0:128]
        p.dma(bqs, bqp_d[:, :])
        p.dma(bks, bk_d[:, :])
        p.dma(bvr, bv_d[:, :])
        p.dma(vms, vm_d[:, :])
        p.mm([(pa[:, 0:128], ONESR[:], bvr, True, True)])
        p.v("tensor_copy", bvb, pa[:, 0:128])
        cast_w(BA[:, 0:8192], wqp_d)
        cast_w(BB[:, 0:1024], wk_d, cols=128)
        cast_w(BB[:, 1024:2048], wv_d, cols=128)
        band = BD[:, 0:384]
        kmt = BD[:, 384:1152]
        p.dma(band[:, :], band_d[:, :])
        xh = BC[:, 0:6144]
        qT = BC[:, 6144:10240]
        kT = BC[:, 10240:11008]
        vas = BC[:, 11008:11788]
        va = vas.rearrange("p (n c) -> p n c", c=130)
        p.v("memset", va[:, :, 64:65], 1.0)
        p.v("memset", va[:, :, 129:130], 1.0)
        pp = SB2[:, 0:384]
        pm = SB2[:, 384:768]
        for sg in range(NSEG):
            base = sg * SEG
            p.dma(kmt[:, :], km_d[:, base:base + SEG])
            p.dma_group([(xh[:, i * 768:(i + 1) * 768],
                          xn0d[i * 128:(i + 1) * 128, base:base + SEG])
                         for i in range(8)])
            for oc in range(8):
                p.mm([(pa[:, 0:512],
                       BA[:, i * 1024 + oc * 128: i * 1024 + (oc + 1) * 128],
                       xh[:, i * 768 + 128: i * 768 + 640], i == 0, i == 7)
                      for i in range(8)])
                p.s("activation", qT[:, oc * 512:(oc + 1) * 512], pa[:, 0:512],
                    AF.Identity, bias=bqs[:, oc:oc + 1])
            for t0, w in ((0, 512), (512, 256)):
                p.mm([(pa[:, 0:w], BB[:, i * 128:(i + 1) * 128],
                       xh[:, i * 768 + t0: i * 768 + t0 + w], i == 0, i == 7)
                      for i in range(8)])
                p.s("activation", F2[:, 0:w], pa[:, 0:w], AF.Identity,
                    bias=bks[:, 0:1])
                p.v("tensor_mul", kT[:, t0:t0 + w], F2[:, 0:w],
                    kmt[:, t0:t0 + w])
            for tt in range(6):
                p.mm([(pv[:, 0:128],
                       xh[:, i * 768 + tt * 128: i * 768 + (tt + 1) * 128],
                       BB[:, 1024 + i * 128:1024 + (i + 1) * 128],
                       i == 0, i == 7) for i in range(8)])
                p.v("tensor_tensor", F2[:, 0:128], pv[:, 0:128], bvb, ALU.add)
                p.v("tensor_scalar_mul", F2[:, 128:256], F2[:, 0:128],
                    vms[:, sg * 6 + tt: sg * 6 + tt + 1])
                p.v("tensor_copy", va[:, tt, 0:64], F2[:, 128:192])
                p.v("tensor_copy", va[:, tt, 65:129], F2[:, 192:256])
            for qb in range(4):
                for o in range(8):
                    for half in range(2):
                        h = o if half == 0 else o + 8
                        pb = half * 64
                        p.mm([(sps[:, kbi * 128:(kbi + 1) * 128],
                               kT[pb:pb + 64, (qb + kbi) * 128:
                                  (qb + kbi + 1) * 128],
                               qT[pb:pb + 64, o * 512 + qb * 128:
                                  o * 512 + (qb + 1) * 128],
                               True, True) for kbi in range(3)])
                        p.s("activation", pp[:], sps[:, 0:384], AF.Exp)
                        p.v("tensor_mul", pm[:], pp[:], band[:, :])
                        p.mm([(ops_[:, 0:65], pm[:, kbi * 128:(kbi + 1) * 128],
                               va[:, qb + kbi, half * 65:(half + 1) * 65],
                               kbi == 0, kbi == 2) for kbi in range(3)])
                        p.v("reciprocal", F2[:, 1024:1025], ops_[:, 64:65])
                        p.v("tensor_scalar_mul", otm[:, h * 64:(h + 1) * 64],
                            ops_[:, 0:64], F2[:, 1024:1025])
                for j in range(8):
                    p.t("transpose", pt[:], otm[:, j * 128:(j + 1) * 128],
                        ID[:])
                    p.v("tensor_copy",
                        BD[:, 1152 + j * 128:1152 + (j + 1) * 128], pt[:])
                p.dma(a0d.rearrange("(c p) t -> p c t", p=128)[
                    :, :, sg * 512 + qb * 128: sg * 512 + (qb + 1) * 128],
                    BD[:, 1152:2176].rearrange("p (c t) -> p c t", t=128),
                    q="gpsimd")

    if on("cmb0"):
        bos = SM[:, 0:8]
        p.dma(bos, bo_d[:, :])
        cast_w(BA[:, 0:8192], wo_d)
        for tt in range(16):
            p.dma_group([(BC[:, i * 512:(i + 1) * 512],
                          a0d[i * 128:(i + 1) * 128, tt * 512:(tt + 1) * 512])
                         for i in range(8)])
            for oc in range(8):
                p.mm([(pa[:, 0:512],
                       BA[:, i * 1024 + oc * 128: i * 1024 + (oc + 1) * 128],
                       BC[:, i * 512:(i + 1) * 512], i == 0, i == 7)
                      for i in range(8)])
                p.s("activation", F2[:, 0:512], pa[:, 0:512], AF.Identity,
                    bias=bos[:, oc:oc + 1])
                p.dma(BD[:, 0:512], g0d[oc * 128:(oc + 1) * 128,
                                        tt * 512:(tt + 1) * 512])
                p.s("activation", F2[:, 512:1024], BD[:, 0:512], AF.Silu)
                p.dma(F1[:, 0:512], xTp[oc * 128:(oc + 1) * 128,
                                        tt * 768 + 128: tt * 768 + 640])
                p.v("tensor_mul", F2[:, 1024:1536], F2[:, 0:512],
                    F2[:, 512:1024])
                p.v("tensor_tensor", F2[:, 1024:1536], F2[:, 1024:1536],
                    F1[:, 0:512], ALU.add)
                p.dma(x1d[oc * 128:(oc + 1) * 128, tt * 512:(tt + 1) * 512],
                      F2[:, 1024:1536], q="gpsimd")

    if on("ln1"):
        def ln1_out(i, t0, w):
            p.dma(xn1d[i * 128:(i + 1) * 128, t0:t0 + w], F2[:, 512:512 + w],
                  q="gpsimd")
            p.v("tensor_copy", BD[:, 0:w], F2[:, 512:512 + w])
            p.dma(xn1bd[i * 128:(i + 1) * 128, t0:t0 + w], BD[:, 0:w],
                  q="gpsimd")
        fm_ln(x1d, T, ln1_out)

    if on("qkv1"):
        cast_w(BA[:, 0:8192], wk1_d)
        cast_w(BB[:, 0:8192], wv1_d)
        for tt in range(16):
            p.dma_group([(BC[:, i * 512:(i + 1) * 512],
                          xn1bd[i * 128:(i + 1) * 128,
                                tt * 512:(tt + 1) * 512])
                         for i in range(8)])
            for oc in range(8):
                p.mm([(pa[:, 0:512],
                       BA[:, i * 1024 + oc * 128: i * 1024 + (oc + 1) * 128],
                       BC[:, i * 512:(i + 1) * 512], i == 0, i == 7)
                      for i in range(8)])
                p.v("tensor_copy", BD[:, 0:512], pa[:, 0:512])
                p.dma(k1d[oc * 128:(oc + 1) * 128, tt * 512:(tt + 1) * 512],
                      BD[:, 0:512], q="gpsimd")
            for j in range(4):
                for hf in range(2):
                    p.mm([(pv[:, 0:512],
                           BC[:, i * 512 + j * 128: i * 512 + (j + 1) * 128],
                           BB[:, i * 1024 + hf * 512:
                              i * 1024 + hf * 512 + 512],
                           i == 0, i == 7) for i in range(8)])
                    p.v("tensor_copy", BD[:, hf * 512:(hf + 1) * 512],
                        pv[:, 0:512])
                p.dma(vbd[:, tt * 4 + j, :], BD[:, 0:1024], q="gpsimd")
        cast_w(BA[:, 0:8192], wq1_d)
        for tt in range(4):
            p.dma_group([(BC[:, i * 512:(i + 1) * 512],
                          xn1bd[i * 128:(i + 1) * 128,
                                tt * 512:(tt + 1) * 512])
                         for i in range(8)])
            for oc in range(8):
                p.mm([(pa[:, 0:512],
                       BA[:, i * 1024 + oc * 128: i * 1024 + (oc + 1) * 128],
                       BC[:, i * 512:(i + 1) * 512], i == 0, i == 7)
                      for i in range(8)])
                p.v("tensor_copy", BD[:, 0:512], pa[:, 0:512])
                p.dma(qTd[oc * 128:(oc + 1) * 128, tt * 512:(tt + 1) * 512],
                      BD[:, 0:512], q="gpsimd")
        wrs = SM[:, 0:64]
        brb = SM[:, 64:72]
        rt = SM[:, 72:80]
        mps = SM[:, 80:81]
        pk = SM[:, 96:112]
        brs = R1[0:1, 128:136]
        p.dma(wrs, wr_d[:, :])
        p.dma(brs, br_d[:, :])
        p.mm([(pa[:, 0:8], ONESR[:], brs, True, True)])
        p.v("tensor_copy", brb, pa[:, 0:8])
        for tt in range(16):
            p.dma_group([(F1[:, i * 128:(i + 1) * 128],
                          xn1d[i * 128:(i + 1) * 128,
                               tt * 128:(tt + 1) * 128])
                         for i in range(8)])
            p.mm([(pr[:, 0:8], F1[:, i * 128:(i + 1) * 128],
                   wrs[:, i * 8:(i + 1) * 8], i == 0, i == 7)
                  for i in range(8)])
            p.v("tensor_tensor", rt, pr[:, 0:8], brb, ALU.add)
            p.s("activation", rt, rt, AF.Sigmoid)
            p.v("reduce_max", mps, rt, AX)
            p.v("tensor_scalar", pk[:, 0:8], rt, mps[:, 0:1], None, ALU.is_ge)
            p.v("tensor_copy", pk[:, 8:9], mps)
            p.v("memset", pk[:, 9:16], 0.0)
            p.t("transpose", pt[0:16, :], pk, ID[:])
            p.v("tensor_copy", RW[:, tt * 128:(tt + 1) * 128], pt[0:16, :])
        p.dma(mskTd[:, :], RW[:, :], q="gpsimd")

    if on("att1"):
        p.dma_group([(BA[:, i * OWN:(i + 1) * OWN],
                      qTd[i * 128:(i + 1) * 128, :]) for i in range(8)])
        kc = BB[:, 0:8192]
        mk = BB[:, 8192:16384]
        vas = BC[:, 0:4160]
        va = vas.rearrange("p (n c) -> p n c", c=65)
        p.v("memset", va[:, :, 64:65], 1.0)
        pp = SB2[:, 0:512]
        pm = SB2[:, 512:1024]
        for ch in range(8):
            p.dma(kc[:], k1d[ch * 128:(ch + 1) * 128, :])
            for c in range(4):
                blocks = kb_blocks(c)
                for h2 in range(2):
                    h = ch * 2 + h2
                    pb = h2 * 64
                    p.dma(mk.rearrange("p (n q) -> p n q", q=512),
                          pmsk_d[:, c * 16:(c + 1) * 16, :])
                    p.dma_group([(va[:, bi * 4:(bi + 1) * 4, 0:64],
                                  vbd[:, lb * 4:(lb + 1) * 4,
                                      h * 64:(h + 1) * 64])
                                 for bi, (lb, masked) in enumerate(blocks)])
                    nkb = 4 * len(blocks)
                    mi = 0
                    kbi = 0
                    for (lb, masked) in blocks:
                        for j in range(4):
                            kcol = lb * 512 + j * 128
                            p.mm([(sps[:, 0:512],
                                   kc[pb:pb + 64, kcol:kcol + 128],
                                   BA[pb:pb + 64,
                                      ch * OWN + c * 512:
                                      ch * OWN + (c + 1) * 512],
                                   True, True)])
                            if masked:
                                p.s("activation", pp[:], sps[:, 0:512],
                                    AF.Exp, scale=0.125)
                                p.v("tensor_mul", pm[:], pp[:],
                                    mk[:, mi * 512:(mi + 1) * 512])
                                mi += 1
                            else:
                                p.s("activation", pm[:], sps[:, 0:512],
                                    AF.Exp, scale=0.125)
                            p.mm([(ops_[:, qq * 65:(qq + 1) * 65],
                                   pm[:, qq * 128:(qq + 1) * 128],
                                   va[:, kbi, 0:65],
                                   kbi == 0 and qq == 0, kbi == nkb - 1)
                                  for qq in range(4)])
                            kbi += 1
                    for qq in range(4):
                        p.v("reciprocal", F2[:, 0:1],
                            ops_[:, qq * 65 + 64:qq * 65 + 65])
                        p.v("tensor_scalar_mul", F2[:, 128:192],
                            ops_[:, qq * 65:qq * 65 + 64], F2[:, 0:1])
                        p.t("transpose", pt[:], F2[:, 128:256], ID[:])
                        p.v("tensor_copy", BD[0:64, 0:128], pt[0:64, :])
                        p.dma(a1d[h * 64:(h + 1) * 64,
                                  c * 512 + qq * 128:
                                  c * 512 + (qq + 1) * 128],
                              BD[0:64, 0:128], q="gpsimd")

    if on("proj1"):
        bps = SM[:, 0:8]
        p.dma(bps, bp_d[:, :])
        cast_w(BA[:, 0:8192], wp_d)
        for tt in range(4):
            p.dma_group([(BC[:, i * 512:(i + 1) * 512],
                          a1d[i * 128:(i + 1) * 128, tt * 512:(tt + 1) * 512])
                         for i in range(8)])
            for oc in range(8):
                p.mm([(pa[:, 0:512],
                       BA[:, i * 1024 + oc * 128: i * 1024 + (oc + 1) * 128],
                       BC[:, i * 512:(i + 1) * 512], i == 0, i == 7)
                      for i in range(8)])
                p.s("activation", F2[:, 0:512], pa[:, 0:512], AF.Identity,
                    bias=bps[:, oc:oc + 1])
                p.dma(F1[:, 0:512], xn1d[oc * 128:(oc + 1) * 128,
                                         tt * 512:(tt + 1) * 512])
                p.v("tensor_tensor", F2[:, 512:1024], F2[:, 0:512],
                    F1[:, 0:512], ALU.add)
                p.dma(y1d[oc * 128:(oc + 1) * 128, tt * 512:(tt + 1) * 512],
                      F2[:, 512:1024], q="gpsimd")

    if on("moe"):
        for s in range(2):
            p.dma_group([(BC[:, i * 1024:(i + 1) * 1024],
                          xn1bd[i * 128:(i + 1) * 128,
                                s * 1024:(s + 1) * 1024])
                         for i in range(8)])
            p.v("memset", BD[:, 0:8192], 0.0)
            for e in range(8):
                p.dma(R1[0:1, 0:1024],
                      mskTd[e:e + 1, s * 1024:(s + 1) * 1024])
                for half in range(2):
                    p.mm([(pa[:, 0:512], ONESR[:],
                           R1[0:1, half * 512:(half + 1) * 512], True, True)])
                    p.v("tensor_copy", SM[:, half * 512:(half + 1) * 512],
                        pa[:, 0:512])
                for fh in range(2):
                    for q4 in range(2):
                        p.dma_group([(F1[:, i * 2048:(i + 1) * 2048],
                                      w1_d[e, (q4 * 4 + i) * 128:
                                           (q4 * 4 + i + 1) * 128,
                                           fh * 2048:(fh + 1) * 2048])
                                     for i in range(4)])
                        p.vgroup([("tensor_copy",
                                   (BA[:, (q4 * 4 + i) * 2048:
                                       (q4 * 4 + i + 1) * 2048],
                                    F1[:, i * 2048:(i + 1) * 2048]), {})
                                  for i in range(4)])
                    for q8 in range(2):
                        p.dma_group([(F1[:, i * 1024:(i + 1) * 1024],
                                      w2_d[e, fh * 2048 + (q8 * 8 + i) * 128:
                                           fh * 2048 + (q8 * 8 + i + 1) * 128,
                                           :])
                                     for i in range(8)])
                        p.vgroup([("tensor_copy",
                                   (BB[:, (q8 * 8 + i) * 1024:
                                       (q8 * 8 + i + 1) * 1024],
                                    F1[:, i * 1024:(i + 1) * 1024]), {})
                                  for i in range(8)])
                    for tt in range(2):
                        for of in range(16):
                            p.mm([(pa[:, 0:512],
                                   BA[:, i * 2048 + of * 128:
                                      i * 2048 + (of + 1) * 128],
                                   BC[:, i * 1024 + tt * 512:
                                      i * 1024 + tt * 512 + 512],
                                   i == 0, i == 7) for i in range(8)])
                            p.s("activation",
                                BC[:, 8192 + of * 512:8192 + (of + 1) * 512],
                                pa[:, 0:512], AF.Gelu)
                        for oc in range(8):
                            p.mm([(pa[:, 0:512],
                                   BB[:, hc * 1024 + oc * 128:
                                      hc * 1024 + (oc + 1) * 128],
                                   BC[:, 8192 + hc * 512:
                                      8192 + (hc + 1) * 512],
                                   hc == 0, hc == 15) for hc in range(16)])
                            p.v("tensor_mul", F2[:, 0:512], pa[:, 0:512],
                                SM[:, tt * 512:(tt + 1) * 512])
                            a = BD[:, oc * 1024 + tt * 512:
                                   oc * 1024 + tt * 512 + 512]
                            p.v("tensor_tensor", a, a, F2[:, 0:512], ALU.add)
            for oc in range(8):
                p.v("tensor_copy", F1[:, 0:1024],
                    BD[:, oc * 1024:(oc + 1) * 1024])
                p.dma(yTd[oc * 128:(oc + 1) * 128, s * 1024:(s + 1) * 1024],
                      F1[:, 0:1024], q="gpsimd")

    if on("fin"):
        gam = SM[:, 0:8]
        bet = SM[:, 8:16]
        p.dma(gam, gam_d[:, :])
        p.dma(bet, bet_d[:, :])

        def fin_out(i, t0, w):
            p.v("tensor_scalar_mul", F2[:, 512:512 + w], F2[:, 512:512 + w],
                gam[:, i:i + 1])
            p.v("tensor_scalar", F2[:, 512:512 + w], F2[:, 512:512 + w],
                bet[:, i:i + 1], None, ALU.add)
            if i == 0:
                p.dma(R1[0:1, 0:w], mskTd[8:9, t0:t0 + w])
                p.mm([(pt[:, 0:w] if False else pr[:, 0:w], ONESR[:],
                       R1[0:1, 0:w], True, True)]) if False else None
                p.mm([(pv[:, 0:w], ONESR[:], R1[0:1, 0:w], True, True)])
                p.v("tensor_copy", F1[:, 4096:4096 + w], pv[:, 0:w])
            p.dma(F1[:, 4608:4608 + w], yTd[i * 128:(i + 1) * 128, t0:t0 + w])
            p.v("tensor_tensor", F1[:, 4608:4608 + w], F1[:, 4608:4608 + w],
                F1[:, 4096:4096 + w], ALU.mult)
            p.s("activation", F1[:, 4608:4608 + w], F1[:, 4608:4608 + w],
                AF.Silu)
            p.v("tensor_tensor", F2[:, 512:512 + w], F2[:, 512:512 + w],
                F1[:, 4608:4608 + w], ALU.mult)
            p.dma(F1[:, 5120:5120 + w], x1d[i * 128:(i + 1) * 128, t0:t0 + w])
            p.v("tensor_tensor", F2[:, 512:512 + w], F2[:, 512:512 + w],
                F1[:, 5120:5120 + w], ALU.add)
            p.dma(outd[i * 128:(i + 1) * 128, t0:t0 + w], F2[:, 512:512 + w],
                  q="gpsimd")
        fm_ln(y1d, OWN, fin_out)

    return p.finish()


def _c(a):
    return np.ascontiguousarray(a, dtype=np.float32)


def _cb(a):
    return np.ascontiguousarray(np.asarray(a).astype(NBF))


def _chunk8(vec):
    return _c(vec.reshape(8, 128).T)


def host_layout(xT_seq, p):
    out = np.zeros((D, LW), np.float32)
    valid = np.zeros(LW, bool)
    for li, g in enumerate(block_order(p)):
        lo = g * 512 - 128
        a, b = max(lo, 0), min(g * 512 + 640, T)
        out[:, li * SEG + (a - lo): li * SEG + (b - lo)] = xT_seq[:, a:b]
        valid[li * SEG + (a - lo): li * SEG + (b - lo)] = True
    return out, valid


def host_masks(p):
    pm = np.zeros((4, 16, 128, 512), np.float32)
    kl = np.arange(128)[:, None]
    ql = np.arange(512)[None, :]
    order = block_order(p)
    for c in range(4):
        qg0 = (4 * c + p) * 512
        mi = 0
        for (lb, masked) in kb_blocks(c):
            for j in range(4):
                if masked:
                    kg = order[lb] * 512 + j * 128
                    pm[c, mi] = (kg + kl) <= (qg0 + ql)
                    mi += 1
        assert mi == 16
    return _cb(pm.reshape(64, 128, 512).transpose(1, 0, 2))


def host_inputs(ins, x, c):
    s, p = c // 4, c % 4
    xT, valid = host_layout(np.ascontiguousarray(x[s].T), p)
    kl = np.arange(128)
    ql = np.arange(128)
    band = np.zeros((128, 384), np.float32)
    for kbi in range(3):
        rel = (kbi * 128 + kl[:, None]) - ql[None, :]
        band[:, kbi * 128:(kbi + 1) * 128] = ((rel >= 0) & (rel < 256))
    return {
        "xTp": xT,
        "km": _cb(np.broadcast_to(valid, (128, LW))),
        "vm": _c(np.ascontiguousarray(valid.reshape(NSEG * 6, 128).T)),
        "pmsk": host_masks(p),
        "band": _cb(band),
        "wqp": _c(ins["L0_attn_Wq"][:, PERM_COLS]),
        "bqp": _chunk8(ins["L0_attn_bq"][PERM_COLS]),
        "wk0": _c(ins["L0_attn_Wk"]),
        "bk0": _c(ins["L0_attn_bk"][:, None]),
        "wv0": _c(ins["L0_attn_Wv"]),
        "bv0": _c(ins["L0_attn_bv"][None, :]),
        "wo": _c(ins["L0_attn_Wo"]),
        "bo": _chunk8(ins["L0_attn_bo"]),
        "wrT": _c(ins["L0_ffn_W_r"].T),
        "wkT": _c(ins["L0_ffn_W_k"].T),
        "wvT": _c(ins["L0_ffn_W_v"].T),
        "mur": _chunk8(ins["L0_ffn_mu_r"]),
        "muk": _chunk8(ins["L0_ffn_mu_k"]),
        "wq1": _c(ins["L1_attn_w_qkv"][:, 0:D]),
        "wk1": _c(ins["L1_attn_w_qkv"][:, D:2 * D]),
        "wv1": _c(ins["L1_attn_w_qkv"][:, 2 * D:3 * D]),
        "wr": _c(np.concatenate([ins["L1_ffn_Wr"][i * 128:(i + 1) * 128]
                                 for i in range(8)], axis=1)),
        "br": _c(ins["L1_ffn_br"][None, :]),
        "wp": _c(ins["L1_attn_w_proj"]),
        "bp": _chunk8(ins["L1_attn_b_proj"]),
        "w1": _c(ins["L1_ffn_W1"]),
        "w2": _c(ins["L1_ffn_W2"]),
        "gam": _chunk8(ins["L1_attn_ln_gamma"]),
        "bet": _chunk8(ins["L1_attn_ln_beta"]),
        "ident": np.eye(128, dtype=np.float32),
    }


_PROG = [None]


def kernel(**inputs):
    ins = {k: np.asarray(v, np.float32) for k, v in inputs.items()}
    x = ins["L0_attn_x"]
    if _PROG[0] is None:
        _PROG[0] = build()
    prog = _PROG[0]
    maps = [host_inputs(ins, x, c) for c in range(8)]
    res = run_bass_kernel_spmd(prog.nc, maps, core_ids=list(range(8)))
    out = np.zeros((2, T, D), np.float32)
    for c in range(8):
        s, p = c // 4, c % 4
        o = res.results[c]["outd"]
        for cc in range(4):
            g = 4 * cc + p
            out[s, g * 512:(g + 1) * 512] = o[:, cc * 512:(cc + 1) * 512].T
    return out


# version 05
# speedup vs baseline: 1.0876x; 1.0876x over previous
import numpy as np
import ml_dtypes
import concourse.bass as bass
import concourse.mybir as mybir
from concourse.bass_utils import run_bass_kernel_spmd

F32 = mybir.dt.float32
BF16 = mybir.dt.bfloat16
AF = mybir.ActivationFunctionType
ALU = mybir.AluOpType
AX = mybir.AxisListType.X
NBF = np.dtype(ml_dtypes.bfloat16)

T, D = 8192, 1024
NSEG, SEGW, SEG = 16, 512, 768
LW = NSEG * SEG
OWN = 2048
ENGS = ("sync", "gpsimd", "vector", "scalar", "tensor")
L0PERM = [h for o in range(8) for h in (o, o + 8)]
PERM_COLS = np.concatenate([np.arange(h * 64, (h + 1) * 64) for h in L0PERM])

STAGE_LO, STAGE_HI = 0, 9
STAGES = ["ln0", "ffn0", "att0", "cmb0", "ln1", "qkv1", "att1", "proj1",
          "moe", "fin"]


def block_order(p):
    own = [4 * c + p for c in range(4)]
    rest = [g for g in range(16) if g % 4 != p]
    return own + rest


def kb_blocks(c):
    out = []
    for lb in range(c + 1):
        out.append((lb, lb == c))
    for j in range(3 * (c + 1)):
        out.append((4 + j, j >= 3 * c))
    return out


class Prog:
    def __init__(self):
        self.nc = bass.Bass("TRN2", target_bir_lowering=False)
        self.ops = []
        self._keep = []

    def din(self, name, shape, dtype=F32):
        return self.nc.dram_tensor(name, shape, dtype, kind="ExternalInput").ap()

    def dhand(self, name, shape, dtype, producer):
        kind = ("ExternalInput" if STAGES.index(producer) < STAGE_LO
                else "ExternalOutput")
        return self.nc.dram_tensor(name, shape, dtype, kind=kind).ap()

    def sb(self, name, shape, dtype):
        return self.nc.alloc_sbuf_tensor(name, shape, dtype)

    def psum(self, name, shape, dtype=F32):
        cm = self.nc.psum_tensor(name, shape, dtype)
        t = cm.__enter__()
        self._keep.append(cm)
        return t

    def _e(self, eng, meth, args, kw):
        self.ops.append(
            (eng, lambda e, m=meth, a=args, k=kw: getattr(e, m)(*a, **k),
             1, False))

    def v(self, meth, *args, **kw):
        self._e("vector", meth, args, kw)

    def s(self, meth, *args, **kw):
        self._e("scalar", meth, args, kw)

    def t(self, meth, *args, **kw):
        self._e("tensor", meth, args, kw)

    def dma(self, out, in_, q="sync"):
        self.ops.append(
            (q, lambda e, o=out, i=in_: e.dma_start(out=o, in_=i), 16, False))

    def dma_group(self, pairs, q="sync"):
        def fn(e, g, pairs=tuple(pairs)):
            for (o, i_) in pairs:
                e.dma_start(out=o, in_=i_).then_inc(g, 16)
        self.ops.append((q, fn, 16 * len(pairs), True))

    def vgroup(self, items):
        def fn(e, items=tuple(items)):
            for (m, a, k) in items:
                ins = getattr(e, m)(*a, **k)
            return ins
        self.ops.append(("vector", fn, 1, False))

    def sgroup(self, items):
        def fn(e, items=tuple(items)):
            for (m, a, k) in items:
                ins = getattr(e, m)(*a, **k)
            return ins
        self.ops.append(("scalar", fn, 1, False))

    def mm(self, items):
        def fn(e, items=tuple(items)):
            for (o, l, r, st, sp) in items:
                ins = e.matmul(o, l, r, start=st, stop=sp)
            return ins
        self.ops.append(("tensor", fn, 1, False))

    def finish(self):
        cm = self.nc.semaphore("g")
        g = cm.__enter__()
        self._keep.append(cm)
        sched = []
        tot = 0
        for eng, fn, inc, selfinc in self.ops:
            sched.append((eng, fn, inc, selfinc, tot))
            tot += inc
        with self.nc.Block() as blk:
            for engname in ENGS:
                @getattr(blk, engname)
                def _(e, engname=engname):
                    for eng, fn, inc, selfinc, before in sched:
                        if eng != engname:
                            continue
                        if before:
                            e.wait_ge(g, before)
                        if selfinc:
                            fn(e, g)
                        else:
                            fn(e).then_inc(g, inc)
                    e.wait_ge(g, tot)
        return self


def build():
    p = Prog()
    lo, hi = STAGE_LO, STAGE_HI

    def on(st):
        return lo <= STAGES.index(st) <= hi

    xTp = p.din("xTp", [D, LW])
    wqp_d = p.din("wqp", [D, D])
    bqp_d = p.din("bqp", [128, 8])
    wk_d = p.din("wk0", [D, 128])
    bk_d = p.din("bk0", [128, 1])
    wv_d = p.din("wv0", [D, 128])
    bv_d = p.din("bv0", [1, 128])
    band_d = p.din("band", [128, 384], BF16)
    km_d = p.din("km", [128, LW], BF16)
    vm_d = p.din("vm", [128, NSEG * 6])
    wo_d = p.din("wo", [D, D])
    bo_d = p.din("bo", [128, 8])
    wrT_d = p.din("wrT", [D, D])
    wkT_d = p.din("wkT", [D, D])
    wvT_d = p.din("wvT", [D, D])
    mur_d = p.din("mur", [128, 8])
    muk_d = p.din("muk", [128, 8])
    wq1_d = p.din("wq1", [D, D])
    wk1_d = p.din("wk1", [D, D])
    wv1_d = p.din("wv1", [D, D])
    wr_d = p.din("wr", [128, 64])
    br_d = p.din("br", [1, 8])
    wp_d = p.din("wp", [D, D])
    bp_d = p.din("bp", [128, 8])
    w1_d = p.din("w1", [8, D, 4096])
    w2_d = p.din("w2", [8, 4096, D])
    gam_d = p.din("gam", [128, 8])
    bet_d = p.din("bet", [128, 8])
    ident_d = p.din("ident", [128, 128])
    pmsk_d = p.din("pmsk", [128, 64, 512], BF16)
    xn0d = p.dhand("xn0d", [D, LW], BF16, "ln0")
    g0d = p.dhand("g0d", [D, T], BF16, "ffn0")
    a0d = p.dhand("a0d", [D, T], BF16, "att0")
    x1d = p.dhand("x1d", [D, T], F32, "cmb0")
    xn1d = p.dhand("xn1d", [D, T], F32, "ln1")
    xn1bd = p.dhand("xn1bd", [D, T], BF16, "ln1")
    k1d = p.dhand("k1d", [D, T], BF16, "qkv1")
    vbd = p.dhand("vbd", [128, 64, D], BF16, "qkv1")
    qTd = p.dhand("qTd", [D, OWN], BF16, "qkv1")
    mskTd = p.dhand("mskTd", [16, OWN], F32, "qkv1")
    a1d = p.dhand("a1d", [D, OWN], BF16, "att1")
    y1d = p.dhand("y1d", [D, OWN], F32, "proj1")
    yTd = p.dhand("yTd", [D, OWN], F32, "moe")
    outd = p.nc.dram_tensor("outd", [D, OWN], F32, kind="ExternalOutput").ap()

    BA = p.sb("BA", [128, 16384], BF16)
    BB = p.sb("BB", [128, 16384], BF16)
    BC = p.sb("BC", [128, 16384], BF16)
    BD = p.sb("BD", [128, 8192], BF16)
    F1 = p.sb("F1", [128, 8192], F32)
    F2 = p.sb("F2", [128, 4096], F32)
    SM = p.sb("SM", [128, 2048], F32)
    RW = p.sb("RW", [16, 2048], F32)
    R1 = p.sb("R1", [1, 1024], F32)
    ST = p.sb("ST", [1, 2048], F32)
    SB2 = p.sb("SB2", [128, 2048], BF16)
    ID = p.sb("ID", [128, 128], F32)
    ONES = p.sb("ONES", [128, 1], F32)
    ONESR = p.sb("ONESR", [1, 128], F32)
    pa = p.psum("pa", [128, 512], F32)
    pv = p.psum("pv", [128, 512], F32)
    sps = p.psum("sps", [128, 512], F32)
    ops_ = p.psum("ops", [128, 260], F32)
    ops2 = p.psum("ops2", [128, 260], F32)
    pt = p.psum("pt", [128, 128], F32)
    pr = p.psum("pr", [128, 128], F32)
    prow = p.psum("prow", [1, 512], F32)

    p.v("memset", ONES[:], 1.0)
    p.v("memset", ONESR[:], 1.0)
    p.v("memset", ST[0:1, 2040:2041], 1e-5)
    p.dma(ID[:], ident_d[:, :])

    def cast_w(dst, src, cols=1024):
        p.dma_group([(F1[:, i * cols:(i + 1) * cols],
                      src[i * 128:(i + 1) * 128, 0:cols]) for i in range(8)])
        p.vgroup([("tensor_copy",
                   (dst[:, i * cols:(i + 1) * cols],
                    F1[:, i * cols:(i + 1) * cols]), {}) for i in range(8)])

    def fm_ln(src_ap, width, out_tiles):
        assert width % 512 == 0
        w = 512
        for t0 in range(0, width, 512):
            p.dma_group([(F1[:, i * w:(i + 1) * w],
                          src_ap[i * 128:(i + 1) * 128, t0:t0 + w])
                         for i in range(8)])
            p.mm([(prow[0:1, 0:w], ONES[:], F1[:, i * w:(i + 1) * w],
                   i == 0, i == 7) for i in range(8)])
            p.v("tensor_scalar_mul", ST[0:1, 0:w], prow[0:1, 0:w], 1.0 / D)
            for i in range(8):
                p.s("activation", F2[:, 0:w], F1[:, i * w:(i + 1) * w],
                    AF.Square)
                p.mm([(prow[0:1, 0:w], ONES[:], F2[:, 0:w], i == 0, i == 7)])
            p.v("tensor_scalar_mul", ST[0:1, 512:512 + w], prow[0:1, 0:w],
                1.0 / D)
            p.v("tensor_mul", ST[0:1, 1024:1024 + w], ST[0:1, 0:w],
                ST[0:1, 0:w])
            p.v("tensor_tensor", ST[0:1, 512:512 + w], ST[0:1, 512:512 + w],
                ST[0:1, 1024:1024 + w], ALU.subtract)
            p.s("activation", ST[0:1, 512:512 + w], ST[0:1, 512:512 + w],
                AF.Ln, bias=ST[0:1, 2040:2041])
            p.s("activation", ST[0:1, 512:512 + w], ST[0:1, 512:512 + w],
                AF.Exp, scale=-0.5)
            p.mm([(pv[:, 0:w], ONESR[:], ST[0:1, 0:w], True, True)])
            p.v("tensor_copy", SM[:, 1024:1024 + w], pv[:, 0:w])
            p.mm([(pv[:, 0:w], ONESR[:], ST[0:1, 512:512 + w], True, True)])
            p.v("tensor_copy", SM[:, 1536:1536 + w], pv[:, 0:w])
            for i in range(8):
                p.v("tensor_tensor", F2[:, 0:w], F1[:, i * w:(i + 1) * w],
                    SM[:, 1024:1024 + w], ALU.subtract)
                p.v("tensor_tensor", F2[:, 512:512 + w], F2[:, 0:w],
                    SM[:, 1536:1536 + w], ALU.mult)
                out_tiles(i, t0, w)

    if on("ln0"):
        def ln0_out(i, t0, w):
            p.v("tensor_copy", BD[:, 0:w], F2[:, 512:512 + w])
            p.dma(xn0d[i * 128:(i + 1) * 128, t0:t0 + w], BD[:, 0:w],
                  q="gpsimd")
        fm_ln(xTp, LW, ln0_out)

    if on("ffn0"):
        mur = SM[:, 0:8]
        muk = SM[:, 8:16]
        m1r = SM[:, 16:24]
        m1k = SM[:, 24:32]
        p.dma(mur, mur_d[:, :])
        p.dma(muk, muk_d[:, :])
        p.v("tensor_scalar", m1r, mur, -1.0, 1.0, ALU.mult, ALU.add)
        p.v("tensor_scalar", m1k, muk, -1.0, 1.0, ALU.mult, ALU.add)
        cast_w(BA[:, 0:8192], wrT_d)
        cast_w(BA[:, 8192:16384], wkT_d)
        cast_w(BB[:, 0:8192], wvT_d)
        xn = BC[:, 0:4104]
        xr = BC[:, 4104:8200]
        xk = BC[:, 8200:12296]
        rf = BD[:, 512:4608]
        for sg in range(NSEG):
            base = sg * SEG
            p.dma_group([(xn[:, i * 513:(i + 1) * 513],
                          xn0d[i * 128:(i + 1) * 128, base + 127:base + 640])
                         for i in range(8)])
            own_i = [xn[:, i * 513 + 1: i * 513 + 513] for i in range(8)]
            prev_i = [xn[:, i * 513: i * 513 + 512] for i in range(8)]
            f2_i = [F2[:, i * 512:(i + 1) * 512] for i in range(8)]
            p.vgroup([("tensor_scalar_mul",
                       (f2_i[i], prev_i[i], m1r[:, i:i + 1]), {})
                      for i in range(8)])
            p.vgroup([("scalar_tensor_tensor",
                       (xr[:, i * 512:(i + 1) * 512], own_i[i],
                        mur[:, i:i + 1], f2_i[i], ALU.mult, ALU.add), {})
                      for i in range(8)])
            p.vgroup([("tensor_scalar_mul",
                       (f2_i[i], prev_i[i], m1k[:, i:i + 1]), {})
                      for i in range(8)])
            p.vgroup([("scalar_tensor_tensor",
                       (xk[:, i * 512:(i + 1) * 512], own_i[i],
                        muk[:, i:i + 1], f2_i[i], ALU.mult, ALU.add), {})
                      for i in range(8)])
            for oc in range(8):
                p.mm([(pa[:, 0:512],
                       BA[:, i * 1024 + oc * 128: i * 1024 + (oc + 1) * 128],
                       xr[:, i * 512:(i + 1) * 512], i == 0, i == 7)
                      for i in range(8)])
                p.s("activation", rf[:, oc * 512:(oc + 1) * 512], pa[:, 0:512],
                    AF.Sigmoid)
            for oc in range(8):
                p.mm([(pa[:, 0:512],
                       BA[:, 8192 + i * 1024 + oc * 128:
                          8192 + i * 1024 + (oc + 1) * 128],
                       xk[:, i * 512:(i + 1) * 512], i == 0, i == 7)
                      for i in range(8)])
                p.s("activation", F2[:, 0:512], pa[:, 0:512], AF.Relu)
                p.s("activation", xr[:, oc * 512:(oc + 1) * 512], F2[:, 0:512],
                    AF.Square)
            for oc in range(8):
                p.mm([(pa[:, 0:512],
                       BB[:, i * 1024 + oc * 128: i * 1024 + (oc + 1) * 128],
                       xr[:, i * 512:(i + 1) * 512], i == 0, i == 7)
                      for i in range(8)])
                p.v("tensor_mul", BD[:, 0:512], pa[:, 0:512],
                    rf[:, oc * 512:(oc + 1) * 512])
                p.dma(g0d[oc * 128:(oc + 1) * 128, sg * 512:(sg + 1) * 512],
                      BD[:, 0:512], q="gpsimd")

    if on("att0"):
        bqs = SM[:, 0:8]
        bks = SM[:, 8:9]
        bvb = SM[:, 128:256]
        vms = SM[:, 256:352]
        otm = SM[:, 1024:2048]
        bvr = R1[0:1, 0:128]
        p.dma(bqs, bqp_d[:, :])
        p.dma(bks, bk_d[:, :])
        p.dma(bvr, bv_d[:, :])
        p.dma(vms, vm_d[:, :])
        p.mm([(pa[:, 0:128], ONESR[:], bvr, True, True)])
        p.v("tensor_copy", bvb, pa[:, 0:128])
        cast_w(BA[:, 0:8192], wqp_d)
        cast_w(BB[:, 0:1024], wk_d, cols=128)
        cast_w(BB[:, 1024:2048], wv_d, cols=128)
        band = BD[:, 0:384]
        kmt = BD[:, 384:1152]
        p.dma(band[:, :], band_d[:, :])
        xh = BC[:, 0:6144]
        qT = BC[:, 6144:10240]
        kT = BC[:, 10240:11008]
        vas = BC[:, 11008:11788]
        va = vas.rearrange("p (n c) -> p n c", c=130)
        p.v("memset", va[:, :, 64:65], 1.0)
        p.v("memset", va[:, :, 129:130], 1.0)
        pp = SB2[:, 0:384]
        pm = SB2[:, 384:768]
        pp2 = SB2[:, 768:1152]
        pm2 = SB2[:, 1152:1536]
        for sg in range(NSEG):
            base = sg * SEG
            p.dma(kmt[:, :], km_d[:, base:base + SEG])
            p.dma_group([(xh[:, i * 768:(i + 1) * 768],
                          xn0d[i * 128:(i + 1) * 128, base:base + SEG])
                         for i in range(8)])
            for oc in range(8):
                p.mm([(pa[:, 0:512],
                       BA[:, i * 1024 + oc * 128: i * 1024 + (oc + 1) * 128],
                       xh[:, i * 768 + 128: i * 768 + 640], i == 0, i == 7)
                      for i in range(8)])
                p.s("activation", qT[:, oc * 512:(oc + 1) * 512], pa[:, 0:512],
                    AF.Identity, bias=bqs[:, oc:oc + 1])
            for t0, w in ((0, 512), (512, 256)):
                p.mm([(pa[:, 0:w], BB[:, i * 128:(i + 1) * 128],
                       xh[:, i * 768 + t0: i * 768 + t0 + w], i == 0, i == 7)
                      for i in range(8)])
                p.s("activation", F2[:, 0:w], pa[:, 0:w], AF.Identity,
                    bias=bks[:, 0:1])
                p.v("tensor_mul", kT[:, t0:t0 + w], F2[:, 0:w],
                    kmt[:, t0:t0 + w])
            for tt in range(6):
                p.mm([(pv[:, 0:128],
                       xh[:, i * 768 + tt * 128: i * 768 + (tt + 1) * 128],
                       BB[:, 1024 + i * 128:1024 + (i + 1) * 128],
                       i == 0, i == 7) for i in range(8)])
                p.v("tensor_tensor", F2[:, 0:128], pv[:, 0:128], bvb, ALU.add)
                p.v("tensor_scalar_mul", F2[:, 128:256], F2[:, 0:128],
                    vms[:, sg * 6 + tt: sg * 6 + tt + 1])
                p.v("tensor_copy", va[:, tt, 0:64], F2[:, 128:192])
                p.v("tensor_copy", va[:, tt, 65:129], F2[:, 192:256])
            for qb in range(4):
                for o in range(8):
                    p.mm([(sps[:, kbi * 128:(kbi + 1) * 128],
                           kT[0:64, (qb + kbi) * 128:(qb + kbi + 1) * 128],
                           qT[0:64, o * 512 + qb * 128:
                              o * 512 + (qb + 1) * 128],
                           True, True) for kbi in range(3)] +
                         [(pv[:, kbi * 128:(kbi + 1) * 128],
                           kT[64:128, (qb + kbi) * 128:(qb + kbi + 1) * 128],
                           qT[64:128, o * 512 + qb * 128:
                              o * 512 + (qb + 1) * 128],
                           True, True) for kbi in range(3)])
                    p.sgroup([("activation", (pp[:], sps[:, 0:384], AF.Exp),
                               {}),
                              ("activation", (pp2[:], pv[:, 0:384], AF.Exp),
                               {})])
                    p.vgroup([("tensor_mul", (pm[:], pp[:], band[:, :]), {}),
                              ("tensor_mul", (pm2[:], pp2[:], band[:, :]),
                               {})])
                    p.mm([(ops_[:, 0:65], pm[:, kbi * 128:(kbi + 1) * 128],
                           va[:, qb + kbi, 0:65],
                           kbi == 0, kbi == 2) for kbi in range(3)] +
                         [(pr[:, 0:65], pm2[:, kbi * 128:(kbi + 1) * 128],
                           va[:, qb + kbi, 65:130],
                           kbi == 0, kbi == 2) for kbi in range(3)])
                    p.vgroup([("reciprocal", (F2[:, 1024:1025],
                                              ops_[:, 64:65]), {}),
                              ("reciprocal", (F2[:, 1025:1026],
                                              pr[:, 64:65]), {})])
                    p.vgroup([("tensor_scalar_mul",
                               (otm[:, o * 64:(o + 1) * 64], ops_[:, 0:64],
                                F2[:, 1024:1025]), {}),
                              ("tensor_scalar_mul",
                               (otm[:, (o + 8) * 64:(o + 9) * 64],
                                pr[:, 0:64], F2[:, 1025:1026]), {})])
                for j in range(8):
                    p.t("transpose", pt[:], otm[:, j * 128:(j + 1) * 128],
                        ID[:])
                    p.v("tensor_copy",
                        BD[:, 1152 + j * 128:1152 + (j + 1) * 128], pt[:])
                p.dma(a0d.rearrange("(c p) t -> p c t", p=128)[
                    :, :, sg * 512 + qb * 128: sg * 512 + (qb + 1) * 128],
                    BD[:, 1152:2176].rearrange("p (c t) -> p c t", t=128),
                    q="gpsimd")

    if on("cmb0"):
        bos = SM[:, 0:8]
        p.dma(bos, bo_d[:, :])
        cast_w(BA[:, 0:8192], wo_d)
        for tt in range(16):
            p.dma_group([(BC[:, i * 512:(i + 1) * 512],
                          a0d[i * 128:(i + 1) * 128, tt * 512:(tt + 1) * 512])
                         for i in range(8)])
            for oc in range(8):
                p.mm([(pa[:, 0:512],
                       BA[:, i * 1024 + oc * 128: i * 1024 + (oc + 1) * 128],
                       BC[:, i * 512:(i + 1) * 512], i == 0, i == 7)
                      for i in range(8)])
                p.s("activation", F2[:, 0:512], pa[:, 0:512], AF.Identity,
                    bias=bos[:, oc:oc + 1])
                p.dma(BD[:, 0:512], g0d[oc * 128:(oc + 1) * 128,
                                        tt * 512:(tt + 1) * 512])
                p.s("activation", F2[:, 512:1024], BD[:, 0:512], AF.Silu)
                p.dma(F1[:, 0:512], xTp[oc * 128:(oc + 1) * 128,
                                        tt * 768 + 128: tt * 768 + 640])
                p.v("tensor_mul", F2[:, 1024:1536], F2[:, 0:512],
                    F2[:, 512:1024])
                p.v("tensor_tensor", F2[:, 1024:1536], F2[:, 1024:1536],
                    F1[:, 0:512], ALU.add)
                p.dma(x1d[oc * 128:(oc + 1) * 128, tt * 512:(tt + 1) * 512],
                      F2[:, 1024:1536], q="gpsimd")

    if on("ln1"):
        def ln1_out(i, t0, w):
            p.dma(xn1d[i * 128:(i + 1) * 128, t0:t0 + w], F2[:, 512:512 + w],
                  q="gpsimd")
            p.v("tensor_copy", BD[:, 0:w], F2[:, 512:512 + w])
            p.dma(xn1bd[i * 128:(i + 1) * 128, t0:t0 + w], BD[:, 0:w],
                  q="gpsimd")
        fm_ln(x1d, T, ln1_out)

    if on("qkv1"):
        cast_w(BA[:, 0:8192], wk1_d)
        cast_w(BB[:, 0:8192], wv1_d)
        for tt in range(16):
            p.dma_group([(BC[:, i * 512:(i + 1) * 512],
                          xn1bd[i * 128:(i + 1) * 128,
                                tt * 512:(tt + 1) * 512])
                         for i in range(8)])
            for oc in range(8):
                p.mm([(pa[:, 0:512],
                       BA[:, i * 1024 + oc * 128: i * 1024 + (oc + 1) * 128],
                       BC[:, i * 512:(i + 1) * 512], i == 0, i == 7)
                      for i in range(8)])
                p.v("tensor_copy", BD[:, 0:512], pa[:, 0:512])
                p.dma(k1d[oc * 128:(oc + 1) * 128, tt * 512:(tt + 1) * 512],
                      BD[:, 0:512], q="gpsimd")
            for j in range(4):
                for hf in range(2):
                    p.mm([(pv[:, 0:512],
                           BC[:, i * 512 + j * 128: i * 512 + (j + 1) * 128],
                           BB[:, i * 1024 + hf * 512:
                              i * 1024 + hf * 512 + 512],
                           i == 0, i == 7) for i in range(8)])
                    p.v("tensor_copy", BD[:, hf * 512:(hf + 1) * 512],
                        pv[:, 0:512])
                p.dma(vbd[:, tt * 4 + j, :], BD[:, 0:1024], q="gpsimd")
        cast_w(BA[:, 0:8192], wq1_d)
        for tt in range(4):
            p.dma_group([(BC[:, i * 512:(i + 1) * 512],
                          xn1bd[i * 128:(i + 1) * 128,
                                tt * 512:(tt + 1) * 512])
                         for i in range(8)])
            for oc in range(8):
                p.mm([(pa[:, 0:512],
                       BA[:, i * 1024 + oc * 128: i * 1024 + (oc + 1) * 128],
                       BC[:, i * 512:(i + 1) * 512], i == 0, i == 7)
                      for i in range(8)])
                p.v("tensor_copy", BD[:, 0:512], pa[:, 0:512])
                p.dma(qTd[oc * 128:(oc + 1) * 128, tt * 512:(tt + 1) * 512],
                      BD[:, 0:512], q="gpsimd")
        wrs = SM[:, 0:64]
        brb = SM[:, 64:72]
        rt = SM[:, 72:80]
        mps = SM[:, 80:81]
        pk = SM[:, 96:112]
        brs = R1[0:1, 128:136]
        p.dma(wrs, wr_d[:, :])
        p.dma(brs, br_d[:, :])
        p.mm([(pa[:, 0:8], ONESR[:], brs, True, True)])
        p.v("tensor_copy", brb, pa[:, 0:8])
        for tt in range(16):
            p.dma_group([(F1[:, i * 128:(i + 1) * 128],
                          xn1d[i * 128:(i + 1) * 128,
                               tt * 128:(tt + 1) * 128])
                         for i in range(8)])
            p.mm([(pr[:, 0:8], F1[:, i * 128:(i + 1) * 128],
                   wrs[:, i * 8:(i + 1) * 8], i == 0, i == 7)
                  for i in range(8)])
            p.v("tensor_tensor", rt, pr[:, 0:8], brb, ALU.add)
            p.s("activation", rt, rt, AF.Sigmoid)
            p.v("reduce_max", mps, rt, AX)
            p.v("tensor_scalar", pk[:, 0:8], rt, mps[:, 0:1], None, ALU.is_ge)
            p.v("tensor_copy", pk[:, 8:9], mps)
            p.v("memset", pk[:, 9:16], 0.0)
            p.t("transpose", pt[0:16, :], pk, ID[:])
            p.v("tensor_copy", RW[:, tt * 128:(tt + 1) * 128], pt[0:16, :])
        p.dma(mskTd[:, :], RW[:, :], q="gpsimd")

    if on("att1"):
        p.dma_group([(BA[:, i * OWN:(i + 1) * OWN],
                      qTd[i * 128:(i + 1) * 128, :]) for i in range(8)])
        kc = BB[:, 0:8192]
        mk = BB[:, 8192:16384]
        va = BC[:, 0:4160].rearrange("p (n c) -> p n c", c=65)
        va2 = BC[:, 4160:8320].rearrange("p (n c) -> p n c", c=65)
        p.v("memset", va[:, :, 64:65], 1.0)
        p.v("memset", va2[:, :, 64:65], 1.0)
        pp = SB2[:, 0:512]
        pm = SB2[:, 512:1024]
        pp2 = SB2[:, 1024:1536]
        pm2 = SB2[:, 1536:2048]
        for ch in range(8):
            p.dma(kc[:], k1d[ch * 128:(ch + 1) * 128, :])
            h0, h1 = ch * 2, ch * 2 + 1
            for c in range(4):
                blocks = kb_blocks(c)
                p.dma(mk.rearrange("p (n q) -> p n q", q=512),
                      pmsk_d[:, c * 16:(c + 1) * 16, :])
                p.dma_group(
                    [(va[:, bi * 4:(bi + 1) * 4, 0:64],
                      vbd[:, lb * 4:(lb + 1) * 4, h0 * 64:(h0 + 1) * 64])
                     for bi, (lb, _) in enumerate(blocks)] +
                    [(va2[:, bi * 4:(bi + 1) * 4, 0:64],
                      vbd[:, lb * 4:(lb + 1) * 4, h1 * 64:(h1 + 1) * 64])
                     for bi, (lb, _) in enumerate(blocks)])
                nkb = 4 * len(blocks)
                mi = 0
                kbi = 0
                qcols = slice(ch * OWN + c * 512, ch * OWN + (c + 1) * 512)
                for (lb, masked) in blocks:
                    for j in range(4):
                        kcol = lb * 512 + j * 128
                        p.mm([(sps[:, 0:512], kc[0:64, kcol:kcol + 128],
                               BA[0:64, qcols], True, True),
                              (pv[:, 0:512], kc[64:128, kcol:kcol + 128],
                               BA[64:128, qcols], True, True)])
                        if masked:
                            p.sgroup([("activation",
                                       (pp[:], sps[:, 0:512], AF.Exp),
                                       {"scale": 0.125}),
                                      ("activation",
                                       (pp2[:], pv[:, 0:512], AF.Exp),
                                       {"scale": 0.125})])
                            p.vgroup([("tensor_mul",
                                       (pm[:], pp[:],
                                        mk[:, mi * 512:(mi + 1) * 512]), {}),
                                      ("tensor_mul",
                                       (pm2[:], pp2[:],
                                        mk[:, mi * 512:(mi + 1) * 512]), {})])
                            mi += 1
                        else:
                            p.sgroup([("activation",
                                       (pm[:], sps[:, 0:512], AF.Exp),
                                       {"scale": 0.125}),
                                      ("activation",
                                       (pm2[:], pv[:, 0:512], AF.Exp),
                                       {"scale": 0.125})])
                        p.mm([(ops_[:, qq * 65:(qq + 1) * 65],
                               pm[:, qq * 128:(qq + 1) * 128],
                               va[:, kbi, 0:65],
                               kbi == 0 and qq == 0, kbi == nkb - 1)
                              for qq in range(4)] +
                             [(ops2[:, qq * 65:(qq + 1) * 65],
                               pm2[:, qq * 128:(qq + 1) * 128],
                               va2[:, kbi, 0:65],
                               kbi == 0 and qq == 0, kbi == nkb - 1)
                              for qq in range(4)])
                        kbi += 1
                for h2, opsx in ((0, ops_), (1, ops2)):
                    h = ch * 2 + h2
                    for qq in range(4):
                        p.v("reciprocal", F2[:, 0:1],
                            opsx[:, qq * 65 + 64:qq * 65 + 65])
                        p.v("tensor_scalar_mul", F2[:, 128:192],
                            opsx[:, qq * 65:qq * 65 + 64], F2[:, 0:1])
                        p.t("transpose", pt[:], F2[:, 128:256], ID[:])
                        p.v("tensor_copy", BD[0:64, 0:128], pt[0:64, :])
                        p.dma(a1d[h * 64:(h + 1) * 64,
                                  c * 512 + qq * 128:
                                  c * 512 + (qq + 1) * 128],
                              BD[0:64, 0:128], q="gpsimd")

    if on("proj1"):
        bps = SM[:, 0:8]
        p.dma(bps, bp_d[:, :])
        cast_w(BA[:, 0:8192], wp_d)
        for tt in range(4):
            p.dma_group([(BC[:, i * 512:(i + 1) * 512],
                          a1d[i * 128:(i + 1) * 128, tt * 512:(tt + 1) * 512])
                         for i in range(8)])
            for oc in range(8):
                p.mm([(pa[:, 0:512],
                       BA[:, i * 1024 + oc * 128: i * 1024 + (oc + 1) * 128],
                       BC[:, i * 512:(i + 1) * 512], i == 0, i == 7)
                      for i in range(8)])
                p.s("activation", F2[:, 0:512], pa[:, 0:512], AF.Identity,
                    bias=bps[:, oc:oc + 1])
                p.dma(F1[:, 0:512], xn1d[oc * 128:(oc + 1) * 128,
                                         tt * 512:(tt + 1) * 512])
                p.v("tensor_tensor", F2[:, 512:1024], F2[:, 0:512],
                    F1[:, 0:512], ALU.add)
                p.dma(y1d[oc * 128:(oc + 1) * 128, tt * 512:(tt + 1) * 512],
                      F2[:, 512:1024], q="gpsimd")

    if on("moe"):
        for s in range(2):
            p.dma_group([(BC[:, i * 1024:(i + 1) * 1024],
                          xn1bd[i * 128:(i + 1) * 128,
                                s * 1024:(s + 1) * 1024])
                         for i in range(8)])
            p.v("memset", BD[:, 0:8192], 0.0)
            for e in range(8):
                p.dma(R1[0:1, 0:1024],
                      mskTd[e:e + 1, s * 1024:(s + 1) * 1024])
                for half in range(2):
                    p.mm([(pa[:, 0:512], ONESR[:],
                           R1[0:1, half * 512:(half + 1) * 512], True, True)])
                    p.v("tensor_copy", SM[:, half * 512:(half + 1) * 512],
                        pa[:, 0:512])
                for fh in range(2):
                    for q4 in range(2):
                        p.dma_group([(F1[:, i * 2048:(i + 1) * 2048],
                                      w1_d[e, (q4 * 4 + i) * 128:
                                           (q4 * 4 + i + 1) * 128,
                                           fh * 2048:(fh + 1) * 2048])
                                     for i in range(4)])
                        p.vgroup([("tensor_copy",
                                   (BA[:, (q4 * 4 + i) * 2048:
                                       (q4 * 4 + i + 1) * 2048],
                                    F1[:, i * 2048:(i + 1) * 2048]), {})
                                  for i in range(4)])
                    for q8 in range(2):
                        p.dma_group([(F1[:, i * 1024:(i + 1) * 1024],
                                      w2_d[e, fh * 2048 + (q8 * 8 + i) * 128:
                                           fh * 2048 + (q8 * 8 + i + 1) * 128,
                                           :])
                                     for i in range(8)])
                        p.vgroup([("tensor_copy",
                                   (BB[:, (q8 * 8 + i) * 1024:
                                       (q8 * 8 + i + 1) * 1024],
                                    F1[:, i * 1024:(i + 1) * 1024]), {})
                                  for i in range(8)])
                    for tt in range(2):
                        for of in range(16):
                            p.mm([(pa[:, 0:512],
                                   BA[:, i * 2048 + of * 128:
                                      i * 2048 + (of + 1) * 128],
                                   BC[:, i * 1024 + tt * 512:
                                      i * 1024 + tt * 512 + 512],
                                   i == 0, i == 7) for i in range(8)])
                            p.s("activation",
                                BC[:, 8192 + of * 512:8192 + (of + 1) * 512],
                                pa[:, 0:512], AF.Gelu)
                        for oc in range(8):
                            p.mm([(pa[:, 0:512],
                                   BB[:, hc * 1024 + oc * 128:
                                      hc * 1024 + (oc + 1) * 128],
                                   BC[:, 8192 + hc * 512:
                                      8192 + (hc + 1) * 512],
                                   hc == 0, hc == 15) for hc in range(16)])
                            p.v("tensor_mul", F2[:, 0:512], pa[:, 0:512],
                                SM[:, tt * 512:(tt + 1) * 512])
                            a = BD[:, oc * 1024 + tt * 512:
                                   oc * 1024 + tt * 512 + 512]
                            p.v("tensor_tensor", a, a, F2[:, 0:512], ALU.add)
            for oc in range(8):
                p.v("tensor_copy", F1[:, 0:1024],
                    BD[:, oc * 1024:(oc + 1) * 1024])
                p.dma(yTd[oc * 128:(oc + 1) * 128, s * 1024:(s + 1) * 1024],
                      F1[:, 0:1024], q="gpsimd")

    if on("fin"):
        gam = SM[:, 0:8]
        bet = SM[:, 8:16]
        p.dma(gam, gam_d[:, :])
        p.dma(bet, bet_d[:, :])

        def fin_out(i, t0, w):
            p.v("tensor_scalar_mul", F2[:, 512:512 + w], F2[:, 512:512 + w],
                gam[:, i:i + 1])
            p.v("tensor_scalar", F2[:, 512:512 + w], F2[:, 512:512 + w],
                bet[:, i:i + 1], None, ALU.add)
            if i == 0:
                p.dma(R1[0:1, 0:w], mskTd[8:9, t0:t0 + w])
                p.mm([(pt[:, 0:w] if False else pr[:, 0:w], ONESR[:],
                       R1[0:1, 0:w], True, True)]) if False else None
                p.mm([(pv[:, 0:w], ONESR[:], R1[0:1, 0:w], True, True)])
                p.v("tensor_copy", F1[:, 4096:4096 + w], pv[:, 0:w])
            p.dma(F1[:, 4608:4608 + w], yTd[i * 128:(i + 1) * 128, t0:t0 + w])
            p.v("tensor_tensor", F1[:, 4608:4608 + w], F1[:, 4608:4608 + w],
                F1[:, 4096:4096 + w], ALU.mult)
            p.s("activation", F1[:, 4608:4608 + w], F1[:, 4608:4608 + w],
                AF.Silu)
            p.v("tensor_tensor", F2[:, 512:512 + w], F2[:, 512:512 + w],
                F1[:, 4608:4608 + w], ALU.mult)
            p.dma(F1[:, 5120:5120 + w], x1d[i * 128:(i + 1) * 128, t0:t0 + w])
            p.v("tensor_tensor", F2[:, 512:512 + w], F2[:, 512:512 + w],
                F1[:, 5120:5120 + w], ALU.add)
            p.dma(outd[i * 128:(i + 1) * 128, t0:t0 + w], F2[:, 512:512 + w],
                  q="gpsimd")
        fm_ln(y1d, OWN, fin_out)

    return p.finish()


def _c(a):
    return np.ascontiguousarray(a, dtype=np.float32)


def _cb(a):
    return np.ascontiguousarray(np.asarray(a).astype(NBF))


def _chunk8(vec):
    return _c(vec.reshape(8, 128).T)


def host_layout(xT_seq, p):
    out = np.zeros((D, LW), np.float32)
    valid = np.zeros(LW, bool)
    for li, g in enumerate(block_order(p)):
        lo = g * 512 - 128
        a, b = max(lo, 0), min(g * 512 + 640, T)
        out[:, li * SEG + (a - lo): li * SEG + (b - lo)] = xT_seq[:, a:b]
        valid[li * SEG + (a - lo): li * SEG + (b - lo)] = True
    return out, valid


def host_masks(p):
    pm = np.zeros((4, 16, 128, 512), np.float32)
    kl = np.arange(128)[:, None]
    ql = np.arange(512)[None, :]
    order = block_order(p)
    for c in range(4):
        qg0 = (4 * c + p) * 512
        mi = 0
        for (lb, masked) in kb_blocks(c):
            for j in range(4):
                if masked:
                    kg = order[lb] * 512 + j * 128
                    pm[c, mi] = (kg + kl) <= (qg0 + ql)
                    mi += 1
        assert mi == 16
    return _cb(pm.reshape(64, 128, 512).transpose(1, 0, 2))


def host_inputs(ins, x, c):
    s, p = c // 4, c % 4
    xT, valid = host_layout(np.ascontiguousarray(x[s].T), p)
    kl = np.arange(128)
    ql = np.arange(128)
    band = np.zeros((128, 384), np.float32)
    for kbi in range(3):
        rel = (kbi * 128 + kl[:, None]) - ql[None, :]
        band[:, kbi * 128:(kbi + 1) * 128] = ((rel >= 0) & (rel < 256))
    return {
        "xTp": xT,
        "km": _cb(np.broadcast_to(valid, (128, LW))),
        "vm": _c(np.ascontiguousarray(valid.reshape(NSEG * 6, 128).T)),
        "pmsk": host_masks(p),
        "band": _cb(band),
        "wqp": _c(ins["L0_attn_Wq"][:, PERM_COLS]),
        "bqp": _chunk8(ins["L0_attn_bq"][PERM_COLS]),
        "wk0": _c(ins["L0_attn_Wk"]),
        "bk0": _c(ins["L0_attn_bk"][:, None]),
        "wv0": _c(ins["L0_attn_Wv"]),
        "bv0": _c(ins["L0_attn_bv"][None, :]),
        "wo": _c(ins["L0_attn_Wo"]),
        "bo": _chunk8(ins["L0_attn_bo"]),
        "wrT": _c(ins["L0_ffn_W_r"].T),
        "wkT": _c(ins["L0_ffn_W_k"].T),
        "wvT": _c(ins["L0_ffn_W_v"].T),
        "mur": _chunk8(ins["L0_ffn_mu_r"]),
        "muk": _chunk8(ins["L0_ffn_mu_k"]),
        "wq1": _c(ins["L1_attn_w_qkv"][:, 0:D]),
        "wk1": _c(ins["L1_attn_w_qkv"][:, D:2 * D]),
        "wv1": _c(ins["L1_attn_w_qkv"][:, 2 * D:3 * D]),
        "wr": _c(np.concatenate([ins["L1_ffn_Wr"][i * 128:(i + 1) * 128]
                                 for i in range(8)], axis=1)),
        "br": _c(ins["L1_ffn_br"][None, :]),
        "wp": _c(ins["L1_attn_w_proj"]),
        "bp": _chunk8(ins["L1_attn_b_proj"]),
        "w1": _c(ins["L1_ffn_W1"]),
        "w2": _c(ins["L1_ffn_W2"]),
        "gam": _chunk8(ins["L1_attn_ln_gamma"]),
        "bet": _chunk8(ins["L1_attn_ln_beta"]),
        "ident": np.eye(128, dtype=np.float32),
    }


_PROG = [None]


def kernel(**inputs):
    ins = {k: np.asarray(v, np.float32) for k, v in inputs.items()}
    x = ins["L0_attn_x"]
    if _PROG[0] is None:
        _PROG[0] = build()
    prog = _PROG[0]
    maps = [host_inputs(ins, x, c) for c in range(8)]
    res = run_bass_kernel_spmd(prog.nc, maps, core_ids=list(range(8)))
    out = np.zeros((2, T, D), np.float32)
    for c in range(8):
        s, p = c // 4, c % 4
        o = res.results[c]["outd"]
        for cc in range(4):
            g = 4 * cc + p
            out[s, g * 512:(g + 1) * 512] = o[:, cc * 512:(cc + 1) * 512].T
    return out


# version 06
# speedup vs baseline: 1.1054x; 1.0163x over previous
import numpy as np
import ml_dtypes
import concourse.bass as bass
import concourse.mybir as mybir
from concourse.bass_utils import run_bass_kernel_spmd

F32 = mybir.dt.float32
BF16 = mybir.dt.bfloat16
AF = mybir.ActivationFunctionType
ALU = mybir.AluOpType
AX = mybir.AxisListType.X
NBF = np.dtype(ml_dtypes.bfloat16)

T, D = 8192, 1024
NSEG, SEGW, SEG = 16, 512, 768
LW = NSEG * SEG
OWN = 2048
ENGS = ("sync", "gpsimd", "vector", "scalar", "tensor")
L0PERM = [h for o in range(8) for h in (o, o + 8)]
PERM_COLS = np.concatenate([np.arange(h * 64, (h + 1) * 64) for h in L0PERM])

STAGE_LO, STAGE_HI = 0, 9
STAGES = ["ln0", "ffn0", "att0", "cmb0", "ln1", "qkv1", "att1", "proj1",
          "moe", "fin"]


def block_order(p):
    own = [4 * c + p for c in range(4)]
    rest = [g for g in range(16) if g % 4 != p]
    return own + rest


def kb_blocks(c):
    out = []
    for lb in range(c + 1):
        out.append((lb, lb == c))
    for j in range(3 * (c + 1)):
        out.append((4 + j, j >= 3 * c))
    return out


class Prog:
    def __init__(self):
        self.nc = bass.Bass("TRN2", target_bir_lowering=False)
        self.ops = []
        self._keep = []

    def din(self, name, shape, dtype=F32):
        return self.nc.dram_tensor(name, shape, dtype, kind="ExternalInput").ap()

    def dhand(self, name, shape, dtype, producer):
        kind = ("ExternalInput" if STAGES.index(producer) < STAGE_LO
                else "ExternalOutput")
        return self.nc.dram_tensor(name, shape, dtype, kind=kind).ap()

    def sb(self, name, shape, dtype):
        return self.nc.alloc_sbuf_tensor(name, shape, dtype)

    def psum(self, name, shape, dtype=F32):
        cm = self.nc.psum_tensor(name, shape, dtype)
        t = cm.__enter__()
        self._keep.append(cm)
        return t

    def _e(self, eng, meth, args, kw):
        self.ops.append(
            (eng, lambda e, m=meth, a=args, k=kw: getattr(e, m)(*a, **k),
             1, False))

    def v(self, meth, *args, **kw):
        self._e("vector", meth, args, kw)

    def s(self, meth, *args, **kw):
        self._e("scalar", meth, args, kw)

    def t(self, meth, *args, **kw):
        self._e("tensor", meth, args, kw)

    def dma(self, out, in_, q="sync"):
        self.ops.append(
            (q, lambda e, o=out, i=in_: e.dma_start(out=o, in_=i), 16, False))

    def dma_group(self, pairs, q="sync"):
        def fn(e, g, pairs=tuple(pairs)):
            for (o, i_) in pairs:
                e.dma_start(out=o, in_=i_).then_inc(g, 16)
        self.ops.append((q, fn, 16 * len(pairs), True))

    def vgroup(self, items):
        def fn(e, items=tuple(items)):
            for (m, a, k) in items:
                ins = getattr(e, m)(*a, **k)
            return ins
        self.ops.append(("vector", fn, 1, False))

    def sgroup(self, items):
        def fn(e, items=tuple(items)):
            for (m, a, k) in items:
                ins = getattr(e, m)(*a, **k)
            return ins
        self.ops.append(("scalar", fn, 1, False))

    def mm(self, items):
        def fn(e, items=tuple(items)):
            for (o, l, r, st, sp) in items:
                ins = e.matmul(o, l, r, start=st, stop=sp)
            return ins
        self.ops.append(("tensor", fn, 1, False))

    def finish(self):
        cm = self.nc.semaphore("g")
        g = cm.__enter__()
        self._keep.append(cm)
        sched = []
        tot = 0
        for eng, fn, inc, selfinc in self.ops:
            sched.append((eng, fn, inc, selfinc, tot))
            tot += inc
        with self.nc.Block() as blk:
            for engname in ENGS:
                @getattr(blk, engname)
                def _(e, engname=engname):
                    for eng, fn, inc, selfinc, before in sched:
                        if eng != engname:
                            continue
                        if before:
                            e.wait_ge(g, before)
                        if selfinc:
                            fn(e, g)
                        else:
                            fn(e).then_inc(g, inc)
                    e.wait_ge(g, tot)
        return self


def build():
    p = Prog()
    lo, hi = STAGE_LO, STAGE_HI

    def on(st):
        return lo <= STAGES.index(st) <= hi

    xTp = p.din("xTp", [D, LW])
    wqp_d = p.din("wqp", [D, D])
    bqp_d = p.din("bqp", [128, 8])
    wk_d = p.din("wk0", [D, 128])
    bk_d = p.din("bk0", [128, 1])
    wv_d = p.din("wv0", [D, 128])
    bv_d = p.din("bv0", [1, 128])
    band_d = p.din("band", [128, 384], BF16)
    km_d = p.din("km", [128, LW], BF16)
    vm_d = p.din("vm", [128, NSEG * 6])
    wo_d = p.din("wo", [D, D])
    bo_d = p.din("bo", [128, 8])
    wrT_d = p.din("wrT", [D, D])
    wkT_d = p.din("wkT", [D, D])
    wvT_d = p.din("wvT", [D, D])
    mur_d = p.din("mur", [128, 8])
    muk_d = p.din("muk", [128, 8])
    wq1_d = p.din("wq1", [D, D])
    wk1_d = p.din("wk1", [D, D])
    wv1_d = p.din("wv1", [D, D])
    wr_d = p.din("wr", [128, 64])
    br_d = p.din("br", [1, 8])
    wp_d = p.din("wp", [D, D])
    bp_d = p.din("bp", [128, 8])
    w1_d = p.din("w1", [8, D, 4096])
    w2_d = p.din("w2", [8, 4096, D])
    gam_d = p.din("gam", [128, 8])
    bet_d = p.din("bet", [128, 8])
    ident_d = p.din("ident", [128, 128])
    pmsk_d = p.din("pmsk", [128, 64, 512], BF16)
    xn0d = p.dhand("xn0d", [D, LW], BF16, "ln0")
    g0d = p.dhand("g0d", [D, T], BF16, "ffn0")
    a0d = p.dhand("a0d", [D, T], BF16, "att0")
    x1d = p.dhand("x1d", [D, T], F32, "cmb0")
    xn1d = p.dhand("xn1d", [D, T], F32, "ln1")
    xn1bd = p.dhand("xn1bd", [D, T], BF16, "ln1")
    k1d = p.dhand("k1d", [D, T], BF16, "qkv1")
    vbd = p.dhand("vbd", [128, 64, D], BF16, "qkv1")
    qTd = p.dhand("qTd", [D, OWN], BF16, "qkv1")
    mskTd = p.dhand("mskTd", [16, OWN], F32, "qkv1")
    a1d = p.dhand("a1d", [D, OWN], BF16, "att1")
    y1d = p.dhand("y1d", [D, OWN], F32, "proj1")
    yTd = p.dhand("yTd", [D, OWN], F32, "moe")
    outd = p.nc.dram_tensor("outd", [D, OWN], F32, kind="ExternalOutput").ap()

    BA = p.sb("BA", [128, 16384], BF16)
    BB = p.sb("BB", [128, 16384], BF16)
    BC = p.sb("BC", [128, 16384], BF16)
    BD = p.sb("BD", [128, 8192], BF16)
    F1 = p.sb("F1", [128, 8192], F32)
    F2 = p.sb("F2", [128, 4096], F32)
    SM = p.sb("SM", [128, 2048], F32)
    RW = p.sb("RW", [16, 2048], F32)
    R1 = p.sb("R1", [1, 1024], F32)
    ST = p.sb("ST", [1, 2048], F32)
    SB2 = p.sb("SB2", [128, 2048], BF16)
    ID = p.sb("ID", [128, 128], F32)
    ONES = p.sb("ONES", [128, 1], F32)
    ONESR = p.sb("ONESR", [1, 128], F32)
    pa = p.psum("pa", [128, 512], F32)
    pv = p.psum("pv", [128, 512], F32)
    sps = p.psum("sps", [128, 512], F32)
    ops_ = p.psum("ops", [128, 260], F32)
    ops2 = p.psum("ops2", [128, 260], F32)
    pt = p.psum("pt", [128, 128], F32)
    pr = p.psum("pr", [128, 128], F32)
    prow = p.psum("prow", [1, 512], F32)

    p.v("memset", ONES[:], 1.0)
    p.v("memset", ONESR[:], 1.0)
    p.v("memset", ST[0:1, 2040:2041], 1e-5)
    p.dma(ID[:], ident_d[:, :])

    def cast_w(dst, src, cols=1024):
        p.dma_group([(F1[:, i * cols:(i + 1) * cols],
                      src[i * 128:(i + 1) * 128, 0:cols]) for i in range(8)])
        p.vgroup([("tensor_copy",
                   (dst[:, i * cols:(i + 1) * cols],
                    F1[:, i * cols:(i + 1) * cols]), {}) for i in range(8)])

    def fm_ln(src_ap, width, out_tiles):
        assert width % 512 == 0
        w = 512
        for t0 in range(0, width, 512):
            p.dma_group([(F1[:, i * w:(i + 1) * w],
                          src_ap[i * 128:(i + 1) * 128, t0:t0 + w])
                         for i in range(8)])
            p.mm([(prow[0:1, 0:w], ONES[:], F1[:, i * w:(i + 1) * w],
                   i == 0, i == 7) for i in range(8)])
            p.v("tensor_scalar_mul", ST[0:1, 0:w], prow[0:1, 0:w], 1.0 / D)
            for i in range(8):
                p.s("activation", F2[:, 0:w], F1[:, i * w:(i + 1) * w],
                    AF.Square)
                p.mm([(prow[0:1, 0:w], ONES[:], F2[:, 0:w], i == 0, i == 7)])
            p.v("tensor_scalar_mul", ST[0:1, 512:512 + w], prow[0:1, 0:w],
                1.0 / D)
            p.v("tensor_mul", ST[0:1, 1024:1024 + w], ST[0:1, 0:w],
                ST[0:1, 0:w])
            p.v("tensor_tensor", ST[0:1, 512:512 + w], ST[0:1, 512:512 + w],
                ST[0:1, 1024:1024 + w], ALU.subtract)
            p.s("activation", ST[0:1, 512:512 + w], ST[0:1, 512:512 + w],
                AF.Ln, bias=ST[0:1, 2040:2041])
            p.s("activation", ST[0:1, 512:512 + w], ST[0:1, 512:512 + w],
                AF.Exp, scale=-0.5)
            p.mm([(pv[:, 0:w], ONESR[:], ST[0:1, 0:w], True, True)])
            p.v("tensor_copy", SM[:, 1024:1024 + w], pv[:, 0:w])
            p.mm([(pv[:, 0:w], ONESR[:], ST[0:1, 512:512 + w], True, True)])
            p.v("tensor_copy", SM[:, 1536:1536 + w], pv[:, 0:w])
            for i in range(8):
                p.v("tensor_tensor", F2[:, 0:w], F1[:, i * w:(i + 1) * w],
                    SM[:, 1024:1024 + w], ALU.subtract)
                p.v("tensor_tensor", F2[:, 512:512 + w], F2[:, 0:w],
                    SM[:, 1536:1536 + w], ALU.mult)
                out_tiles(i, t0, w)

    if on("ln0"):
        def ln0_out(i, t0, w):
            p.v("tensor_copy", BD[:, 0:w], F2[:, 512:512 + w])
            p.dma(xn0d[i * 128:(i + 1) * 128, t0:t0 + w], BD[:, 0:w],
                  q="gpsimd")
        fm_ln(xTp, LW, ln0_out)

    if on("ffn0"):
        mur = SM[:, 0:8]
        muk = SM[:, 8:16]
        m1r = SM[:, 16:24]
        m1k = SM[:, 24:32]
        p.dma(mur, mur_d[:, :])
        p.dma(muk, muk_d[:, :])
        p.v("tensor_scalar", m1r, mur, -1.0, 1.0, ALU.mult, ALU.add)
        p.v("tensor_scalar", m1k, muk, -1.0, 1.0, ALU.mult, ALU.add)
        cast_w(BA[:, 0:8192], wrT_d)
        cast_w(BA[:, 8192:16384], wkT_d)
        cast_w(BB[:, 0:8192], wvT_d)
        xn = BC[:, 0:4104]
        xr = BC[:, 4104:8200]
        xk = BC[:, 8200:12296]
        rf = BD[:, 512:4608]
        for sg in range(NSEG):
            base = sg * SEG
            p.dma_group([(xn[:, i * 513:(i + 1) * 513],
                          xn0d[i * 128:(i + 1) * 128, base + 127:base + 640])
                         for i in range(8)])
            own_i = [xn[:, i * 513 + 1: i * 513 + 513] for i in range(8)]
            prev_i = [xn[:, i * 513: i * 513 + 512] for i in range(8)]
            f2_i = [F2[:, i * 512:(i + 1) * 512] for i in range(8)]
            p.vgroup([("tensor_scalar_mul",
                       (f2_i[i], prev_i[i], m1r[:, i:i + 1]), {})
                      for i in range(8)])
            p.vgroup([("scalar_tensor_tensor",
                       (xr[:, i * 512:(i + 1) * 512], own_i[i],
                        mur[:, i:i + 1], f2_i[i], ALU.mult, ALU.add), {})
                      for i in range(8)])
            p.vgroup([("tensor_scalar_mul",
                       (f2_i[i], prev_i[i], m1k[:, i:i + 1]), {})
                      for i in range(8)])
            p.vgroup([("scalar_tensor_tensor",
                       (xk[:, i * 512:(i + 1) * 512], own_i[i],
                        muk[:, i:i + 1], f2_i[i], ALU.mult, ALU.add), {})
                      for i in range(8)])
            for oc in range(8):
                p.mm([(pa[:, 0:512],
                       BA[:, i * 1024 + oc * 128: i * 1024 + (oc + 1) * 128],
                       xr[:, i * 512:(i + 1) * 512], i == 0, i == 7)
                      for i in range(8)])
                p.s("activation", rf[:, oc * 512:(oc + 1) * 512], pa[:, 0:512],
                    AF.Sigmoid)
            for oc in range(8):
                p.mm([(pa[:, 0:512],
                       BA[:, 8192 + i * 1024 + oc * 128:
                          8192 + i * 1024 + (oc + 1) * 128],
                       xk[:, i * 512:(i + 1) * 512], i == 0, i == 7)
                      for i in range(8)])
                p.s("activation", F2[:, 0:512], pa[:, 0:512], AF.Relu)
                p.s("activation", xr[:, oc * 512:(oc + 1) * 512], F2[:, 0:512],
                    AF.Square)
            for oc in range(8):
                p.mm([(pa[:, 0:512],
                       BB[:, i * 1024 + oc * 128: i * 1024 + (oc + 1) * 128],
                       xr[:, i * 512:(i + 1) * 512], i == 0, i == 7)
                      for i in range(8)])
                p.v("tensor_mul", BD[:, 0:512], pa[:, 0:512],
                    rf[:, oc * 512:(oc + 1) * 512])
                p.dma(g0d[oc * 128:(oc + 1) * 128, sg * 512:(sg + 1) * 512],
                      BD[:, 0:512], q="gpsimd")

    if on("att0"):
        bqs = SM[:, 0:8]
        bks = SM[:, 8:9]
        bvb = SM[:, 128:256]
        vms = SM[:, 256:352]
        otm = SM[:, 1024:2048]
        bvr = R1[0:1, 0:128]
        p.dma(bqs, bqp_d[:, :])
        p.dma(bks, bk_d[:, :])
        p.dma(bvr, bv_d[:, :])
        p.dma(vms, vm_d[:, :])
        p.mm([(pa[:, 0:128], ONESR[:], bvr, True, True)])
        p.v("tensor_copy", bvb, pa[:, 0:128])
        cast_w(BA[:, 0:8192], wqp_d)
        cast_w(BB[:, 0:1024], wk_d, cols=128)
        cast_w(BB[:, 1024:2048], wv_d, cols=128)
        band = BD[:, 0:384]
        kmt = BD[:, 384:1152]
        p.dma(band[:, :], band_d[:, :])
        xh = BC[:, 0:6144]
        qT = BC[:, 6144:10240]
        kT = BC[:, 10240:11008]
        vas = BC[:, 11008:11788]
        va = vas.rearrange("p (n c) -> p n c", c=130)
        p.v("memset", va[:, :, 64:65], 1.0)
        p.v("memset", va[:, :, 129:130], 1.0)
        pp = SB2[:, 0:384]
        pm = SB2[:, 384:768]
        pp2 = SB2[:, 768:1152]
        pm2 = SB2[:, 1152:1536]
        for sg in range(NSEG):
            base = sg * SEG
            p.dma(kmt[:, :], km_d[:, base:base + SEG])
            p.dma_group([(xh[:, i * 768:(i + 1) * 768],
                          xn0d[i * 128:(i + 1) * 128, base:base + SEG])
                         for i in range(8)])
            for oc in range(8):
                p.mm([(pa[:, 0:512],
                       BA[:, i * 1024 + oc * 128: i * 1024 + (oc + 1) * 128],
                       xh[:, i * 768 + 128: i * 768 + 640], i == 0, i == 7)
                      for i in range(8)])
                p.s("activation", qT[:, oc * 512:(oc + 1) * 512], pa[:, 0:512],
                    AF.Identity, bias=bqs[:, oc:oc + 1])
            for t0, w in ((0, 512), (512, 256)):
                p.mm([(pa[:, 0:w], BB[:, i * 128:(i + 1) * 128],
                       xh[:, i * 768 + t0: i * 768 + t0 + w], i == 0, i == 7)
                      for i in range(8)])
                p.s("activation", F2[:, 0:w], pa[:, 0:w], AF.Identity,
                    bias=bks[:, 0:1])
                p.v("tensor_mul", kT[:, t0:t0 + w], F2[:, 0:w],
                    kmt[:, t0:t0 + w])
            for tt in range(6):
                p.mm([(pv[:, 0:128],
                       xh[:, i * 768 + tt * 128: i * 768 + (tt + 1) * 128],
                       BB[:, 1024 + i * 128:1024 + (i + 1) * 128],
                       i == 0, i == 7) for i in range(8)])
                p.v("tensor_tensor", F2[:, 0:128], pv[:, 0:128], bvb, ALU.add)
                p.v("tensor_scalar_mul", F2[:, 128:256], F2[:, 0:128],
                    vms[:, sg * 6 + tt: sg * 6 + tt + 1])
                p.v("tensor_copy", va[:, tt, 0:64], F2[:, 128:192])
                p.v("tensor_copy", va[:, tt, 65:129], F2[:, 192:256])
            for qb in range(4):
                for o in range(8):
                    p.mm([(sps[:, kbi * 128:(kbi + 1) * 128],
                           kT[0:64, (qb + kbi) * 128:(qb + kbi + 1) * 128],
                           qT[0:64, o * 512 + qb * 128:
                              o * 512 + (qb + 1) * 128],
                           True, True) for kbi in range(3)] +
                         [(pv[:, kbi * 128:(kbi + 1) * 128],
                           kT[64:128, (qb + kbi) * 128:(qb + kbi + 1) * 128],
                           qT[64:128, o * 512 + qb * 128:
                              o * 512 + (qb + 1) * 128],
                           True, True) for kbi in range(3)])
                    p.sgroup([("activation", (pp[:], sps[:, 0:384], AF.Exp),
                               {}),
                              ("activation", (pp2[:], pv[:, 0:384], AF.Exp),
                               {})])
                    p.vgroup([("tensor_mul", (pm[:], pp[:], band[:, :]), {}),
                              ("tensor_mul", (pm2[:], pp2[:], band[:, :]),
                               {})])
                    p.mm([(ops_[:, 0:65], pm[:, kbi * 128:(kbi + 1) * 128],
                           va[:, qb + kbi, 0:65],
                           kbi == 0, kbi == 2) for kbi in range(3)] +
                         [(pr[:, 0:65], pm2[:, kbi * 128:(kbi + 1) * 128],
                           va[:, qb + kbi, 65:130],
                           kbi == 0, kbi == 2) for kbi in range(3)])
                    p.vgroup([("reciprocal", (F2[:, 1024:1025],
                                              ops_[:, 64:65]), {}),
                              ("reciprocal", (F2[:, 1025:1026],
                                              pr[:, 64:65]), {})])
                    p.vgroup([("tensor_scalar_mul",
                               (otm[:, o * 64:(o + 1) * 64], ops_[:, 0:64],
                                F2[:, 1024:1025]), {}),
                              ("tensor_scalar_mul",
                               (otm[:, (o + 8) * 64:(o + 9) * 64],
                                pr[:, 0:64], F2[:, 1025:1026]), {})])
                for j in range(8):
                    p.t("transpose", pt[:], otm[:, j * 128:(j + 1) * 128],
                        ID[:])
                    p.v("tensor_copy",
                        BD[:, 1152 + j * 128:1152 + (j + 1) * 128], pt[:])
                p.dma(a0d.rearrange("(c p) t -> p c t", p=128)[
                    :, :, sg * 512 + qb * 128: sg * 512 + (qb + 1) * 128],
                    BD[:, 1152:2176].rearrange("p (c t) -> p c t", t=128),
                    q="gpsimd")

    if on("cmb0"):
        bos = SM[:, 0:8]
        p.dma(bos, bo_d[:, :])
        cast_w(BA[:, 0:8192], wo_d)
        for tt in range(16):
            p.dma_group([(BC[:, i * 512:(i + 1) * 512],
                          a0d[i * 128:(i + 1) * 128, tt * 512:(tt + 1) * 512])
                         for i in range(8)])
            for oc in range(8):
                p.mm([(pa[:, 0:512],
                       BA[:, i * 1024 + oc * 128: i * 1024 + (oc + 1) * 128],
                       BC[:, i * 512:(i + 1) * 512], i == 0, i == 7)
                      for i in range(8)])
                p.s("activation", F2[:, 0:512], pa[:, 0:512], AF.Identity,
                    bias=bos[:, oc:oc + 1])
                p.dma(BD[:, 0:512], g0d[oc * 128:(oc + 1) * 128,
                                        tt * 512:(tt + 1) * 512])
                p.s("activation", F2[:, 512:1024], BD[:, 0:512], AF.Silu)
                p.dma(F1[:, 0:512], xTp[oc * 128:(oc + 1) * 128,
                                        tt * 768 + 128: tt * 768 + 640])
                p.v("tensor_mul", F2[:, 1024:1536], F2[:, 0:512],
                    F2[:, 512:1024])
                p.v("tensor_tensor", F2[:, 1024:1536], F2[:, 1024:1536],
                    F1[:, 0:512], ALU.add)
                p.dma(x1d[oc * 128:(oc + 1) * 128, tt * 512:(tt + 1) * 512],
                      F2[:, 1024:1536], q="gpsimd")

    if on("ln1"):
        def ln1_out(i, t0, w):
            p.dma(xn1d[i * 128:(i + 1) * 128, t0:t0 + w], F2[:, 512:512 + w],
                  q="gpsimd")
            p.v("tensor_copy", BD[:, 0:w], F2[:, 512:512 + w])
            p.dma(xn1bd[i * 128:(i + 1) * 128, t0:t0 + w], BD[:, 0:w],
                  q="gpsimd")
        fm_ln(x1d, T, ln1_out)

    if on("qkv1"):
        cast_w(BA[:, 0:8192], wk1_d)
        cast_w(BB[:, 0:8192], wv1_d)
        for tt in range(16):
            p.dma_group([(BC[:, i * 512:(i + 1) * 512],
                          xn1bd[i * 128:(i + 1) * 128,
                                tt * 512:(tt + 1) * 512])
                         for i in range(8)])
            for oc in range(8):
                p.mm([(pa[:, 0:512],
                       BA[:, i * 1024 + oc * 128: i * 1024 + (oc + 1) * 128],
                       BC[:, i * 512:(i + 1) * 512], i == 0, i == 7)
                      for i in range(8)])
                p.v("tensor_copy", BD[:, 0:512], pa[:, 0:512])
                p.dma(k1d[oc * 128:(oc + 1) * 128, tt * 512:(tt + 1) * 512],
                      BD[:, 0:512], q="gpsimd")
            for j in range(4):
                for hf in range(2):
                    p.mm([(pv[:, 0:512],
                           BC[:, i * 512 + j * 128: i * 512 + (j + 1) * 128],
                           BB[:, i * 1024 + hf * 512:
                              i * 1024 + hf * 512 + 512],
                           i == 0, i == 7) for i in range(8)])
                    p.v("tensor_copy", BD[:, hf * 512:(hf + 1) * 512],
                        pv[:, 0:512])
                p.dma(vbd[:, tt * 4 + j, :], BD[:, 0:1024], q="gpsimd")
        cast_w(BA[:, 0:8192], wq1_d)
        for tt in range(4):
            p.dma_group([(BC[:, i * 512:(i + 1) * 512],
                          xn1bd[i * 128:(i + 1) * 128,
                                tt * 512:(tt + 1) * 512])
                         for i in range(8)])
            for oc in range(8):
                p.mm([(pa[:, 0:512],
                       BA[:, i * 1024 + oc * 128: i * 1024 + (oc + 1) * 128],
                       BC[:, i * 512:(i + 1) * 512], i == 0, i == 7)
                      for i in range(8)])
                p.v("tensor_copy", BD[:, 0:512], pa[:, 0:512])
                p.dma(qTd[oc * 128:(oc + 1) * 128, tt * 512:(tt + 1) * 512],
                      BD[:, 0:512], q="gpsimd")
        wrs = SM[:, 0:64]
        brb = SM[:, 64:72]
        rt = SM[:, 72:80]
        mps = SM[:, 80:81]
        pk = SM[:, 96:112]
        brs = R1[0:1, 128:136]
        p.dma(wrs, wr_d[:, :])
        p.dma(brs, br_d[:, :])
        p.mm([(pa[:, 0:8], ONESR[:], brs, True, True)])
        p.v("tensor_copy", brb, pa[:, 0:8])
        for tt in range(16):
            p.dma_group([(F1[:, i * 128:(i + 1) * 128],
                          xn1d[i * 128:(i + 1) * 128,
                               tt * 128:(tt + 1) * 128])
                         for i in range(8)])
            p.mm([(pr[:, 0:8], F1[:, i * 128:(i + 1) * 128],
                   wrs[:, i * 8:(i + 1) * 8], i == 0, i == 7)
                  for i in range(8)])
            p.v("tensor_tensor", rt, pr[:, 0:8], brb, ALU.add)
            p.s("activation", rt, rt, AF.Sigmoid)
            p.v("reduce_max", mps, rt, AX)
            p.v("tensor_scalar", pk[:, 0:8], rt, mps[:, 0:1], None, ALU.is_ge)
            p.v("tensor_copy", pk[:, 8:9], mps)
            p.v("memset", pk[:, 9:16], 0.0)
            p.t("transpose", pt[0:16, :], pk, ID[:])
            p.v("tensor_copy", RW[:, tt * 128:(tt + 1) * 128], pt[0:16, :])
        p.dma(mskTd[:, :], RW[:, :], q="gpsimd")

    if on("att1"):
        p.dma_group([(BA[:, i * OWN:(i + 1) * OWN],
                      qTd[i * 128:(i + 1) * 128, :]) for i in range(8)])
        kc = BB[:, 0:8192]
        mk = BB[:, 8192:16384]
        va = BC[:, 0:4160].rearrange("p (n c) -> p n c", c=65)
        va2 = BC[:, 4160:8320].rearrange("p (n c) -> p n c", c=65)
        p.v("memset", va[:, :, 64:65], 1.0)
        p.v("memset", va2[:, :, 64:65], 1.0)
        pp = SB2[:, 0:512]
        pm = SB2[:, 512:1024]
        pp2 = SB2[:, 1024:1536]
        pm2 = SB2[:, 1536:2048]
        for ch in range(8):
            p.dma(kc[:], k1d[ch * 128:(ch + 1) * 128, :])
            h0, h1 = ch * 2, ch * 2 + 1
            for c in range(4):
                blocks = kb_blocks(c)
                p.dma(mk.rearrange("p (n q) -> p n q", q=512),
                      pmsk_d[:, c * 16:(c + 1) * 16, :])
                p.dma_group(
                    [(va[:, bi * 4:(bi + 1) * 4, 0:64],
                      vbd[:, lb * 4:(lb + 1) * 4, h0 * 64:(h0 + 1) * 64])
                     for bi, (lb, _) in enumerate(blocks)] +
                    [(va2[:, bi * 4:(bi + 1) * 4, 0:64],
                      vbd[:, lb * 4:(lb + 1) * 4, h1 * 64:(h1 + 1) * 64])
                     for bi, (lb, _) in enumerate(blocks)])
                nkb = 4 * len(blocks)
                mi = 0
                kbi = 0
                qcols = slice(ch * OWN + c * 512, ch * OWN + (c + 1) * 512)
                for (lb, masked) in blocks:
                    for j in range(4):
                        kcol = lb * 512 + j * 128
                        p.mm([(sps[:, 0:512], kc[0:64, kcol:kcol + 128],
                               BA[0:64, qcols], True, True),
                              (pv[:, 0:512], kc[64:128, kcol:kcol + 128],
                               BA[64:128, qcols], True, True)])
                        if masked:
                            p.sgroup([("activation",
                                       (pp[:], sps[:, 0:512], AF.Exp),
                                       {"scale": 0.125}),
                                      ("activation",
                                       (pp2[:], pv[:, 0:512], AF.Exp),
                                       {"scale": 0.125})])
                            p.vgroup([("tensor_mul",
                                       (pm[:], pp[:],
                                        mk[:, mi * 512:(mi + 1) * 512]), {}),
                                      ("tensor_mul",
                                       (pm2[:], pp2[:],
                                        mk[:, mi * 512:(mi + 1) * 512]), {})])
                            mi += 1
                        else:
                            p.sgroup([("activation",
                                       (pm[:], sps[:, 0:512], AF.Exp),
                                       {"scale": 0.125}),
                                      ("activation",
                                       (pm2[:], pv[:, 0:512], AF.Exp),
                                       {"scale": 0.125})])
                        p.mm([(ops_[:, qq * 65:(qq + 1) * 65],
                               pm[:, qq * 128:(qq + 1) * 128],
                               va[:, kbi, 0:65],
                               kbi == 0 and qq == 0, kbi == nkb - 1)
                              for qq in range(4)] +
                             [(ops2[:, qq * 65:(qq + 1) * 65],
                               pm2[:, qq * 128:(qq + 1) * 128],
                               va2[:, kbi, 0:65],
                               kbi == 0 and qq == 0, kbi == nkb - 1)
                              for qq in range(4)])
                        kbi += 1
                for h2, opsx in ((0, ops_), (1, ops2)):
                    h = ch * 2 + h2
                    for qq in range(4):
                        p.v("reciprocal", F2[:, 0:1],
                            opsx[:, qq * 65 + 64:qq * 65 + 65])
                        p.v("tensor_scalar_mul", F2[:, 128:192],
                            opsx[:, qq * 65:qq * 65 + 64], F2[:, 0:1])
                        p.t("transpose", pt[:], F2[:, 128:256], ID[:])
                        p.v("tensor_copy", BD[0:64, 0:128], pt[0:64, :])
                        p.dma(a1d[h * 64:(h + 1) * 64,
                                  c * 512 + qq * 128:
                                  c * 512 + (qq + 1) * 128],
                              BD[0:64, 0:128], q="gpsimd")

    if on("proj1"):
        bps = SM[:, 0:8]
        p.dma(bps, bp_d[:, :])
        cast_w(BA[:, 0:8192], wp_d)
        for tt in range(4):
            p.dma_group([(BC[:, i * 512:(i + 1) * 512],
                          a1d[i * 128:(i + 1) * 128, tt * 512:(tt + 1) * 512])
                         for i in range(8)])
            for oc in range(8):
                p.mm([(pa[:, 0:512],
                       BA[:, i * 1024 + oc * 128: i * 1024 + (oc + 1) * 128],
                       BC[:, i * 512:(i + 1) * 512], i == 0, i == 7)
                      for i in range(8)])
                p.s("activation", F2[:, 0:512], pa[:, 0:512], AF.Identity,
                    bias=bps[:, oc:oc + 1])
                p.dma(F1[:, 0:512], xn1d[oc * 128:(oc + 1) * 128,
                                         tt * 512:(tt + 1) * 512])
                p.v("tensor_tensor", F2[:, 512:1024], F2[:, 0:512],
                    F1[:, 0:512], ALU.add)
                p.dma(y1d[oc * 128:(oc + 1) * 128, tt * 512:(tt + 1) * 512],
                      F2[:, 512:1024], q="gpsimd")

    if on("moe"):
        for s in range(2):
            p.dma_group([(BC[:, i * 1024:(i + 1) * 1024],
                          xn1bd[i * 128:(i + 1) * 128,
                                s * 1024:(s + 1) * 1024])
                         for i in range(8)])
            p.v("memset", BD[:, 0:8192], 0.0)
            for e in range(8):
                p.dma(R1[0:1, 0:1024],
                      mskTd[e:e + 1, s * 1024:(s + 1) * 1024])
                for half in range(2):
                    p.mm([(pa[:, 0:512], ONESR[:],
                           R1[0:1, half * 512:(half + 1) * 512], True, True)])
                    p.v("tensor_copy", SM[:, half * 512:(half + 1) * 512],
                        pa[:, 0:512])
                for fh in range(2):
                    for q4 in range(2):
                        p.dma_group([(F1[:, i * 2048:(i + 1) * 2048],
                                      w1_d[e, (q4 * 4 + i) * 128:
                                           (q4 * 4 + i + 1) * 128,
                                           fh * 2048:(fh + 1) * 2048])
                                     for i in range(4)])
                        p.vgroup([("tensor_copy",
                                   (BA[:, (q4 * 4 + i) * 2048:
                                       (q4 * 4 + i + 1) * 2048],
                                    F1[:, i * 2048:(i + 1) * 2048]), {})
                                  for i in range(4)])
                    for q8 in range(2):
                        p.dma_group([(F1[:, i * 1024:(i + 1) * 1024],
                                      w2_d[e, fh * 2048 + (q8 * 8 + i) * 128:
                                           fh * 2048 + (q8 * 8 + i + 1) * 128,
                                           :])
                                     for i in range(8)])
                        p.vgroup([("tensor_copy",
                                   (BB[:, (q8 * 8 + i) * 1024:
                                       (q8 * 8 + i + 1) * 1024],
                                    F1[:, i * 1024:(i + 1) * 1024]), {})
                                  for i in range(8)])
                    for tt in range(2):
                        for of in range(0, 16, 2):
                            p.mm([(pa[:, 0:512],
                                   BA[:, i * 2048 + of * 128:
                                      i * 2048 + (of + 1) * 128],
                                   BC[:, i * 1024 + tt * 512:
                                      i * 1024 + tt * 512 + 512],
                                   i == 0, i == 7) for i in range(8)] +
                                 [(sps[:, 0:512],
                                   BA[:, i * 2048 + (of + 1) * 128:
                                      i * 2048 + (of + 2) * 128],
                                   BC[:, i * 1024 + tt * 512:
                                      i * 1024 + tt * 512 + 512],
                                   i == 0, i == 7) for i in range(8)])
                            p.sgroup([("activation",
                                       (BC[:, 8192 + of * 512:
                                           8192 + (of + 1) * 512],
                                        pa[:, 0:512], AF.Gelu), {}),
                                      ("activation",
                                       (BC[:, 8192 + (of + 1) * 512:
                                           8192 + (of + 2) * 512],
                                        sps[:, 0:512], AF.Gelu), {})])
                        for oc in range(0, 8, 2):
                            p.mm([(pa[:, 0:512],
                                   BB[:, hc * 1024 + oc * 128:
                                      hc * 1024 + (oc + 1) * 128],
                                   BC[:, 8192 + hc * 512:
                                      8192 + (hc + 1) * 512],
                                   hc == 0, hc == 15) for hc in range(16)] +
                                 [(sps[:, 0:512],
                                   BB[:, hc * 1024 + (oc + 1) * 128:
                                      hc * 1024 + (oc + 2) * 128],
                                   BC[:, 8192 + hc * 512:
                                      8192 + (hc + 1) * 512],
                                   hc == 0, hc == 15) for hc in range(16)])
                            p.vgroup([("tensor_mul",
                                       (F2[:, 0:512], pa[:, 0:512],
                                        SM[:, tt * 512:(tt + 1) * 512]), {}),
                                      ("tensor_mul",
                                       (F2[:, 512:1024], sps[:, 0:512],
                                        SM[:, tt * 512:(tt + 1) * 512]), {})])
                            a0 = BD[:, oc * 1024 + tt * 512:
                                    oc * 1024 + tt * 512 + 512]
                            a1 = BD[:, (oc + 1) * 1024 + tt * 512:
                                    (oc + 1) * 1024 + tt * 512 + 512]
                            p.vgroup([("tensor_tensor",
                                       (a0, a0, F2[:, 0:512], ALU.add), {}),
                                      ("tensor_tensor",
                                       (a1, a1, F2[:, 512:1024], ALU.add),
                                       {})])
            for oc in range(8):
                p.v("tensor_copy", F1[:, 0:1024],
                    BD[:, oc * 1024:(oc + 1) * 1024])
                p.dma(yTd[oc * 128:(oc + 1) * 128, s * 1024:(s + 1) * 1024],
                      F1[:, 0:1024], q="gpsimd")

    if on("fin"):
        gam = SM[:, 0:8]
        bet = SM[:, 8:16]
        p.dma(gam, gam_d[:, :])
        p.dma(bet, bet_d[:, :])

        def fin_out(i, t0, w):
            p.v("tensor_scalar_mul", F2[:, 512:512 + w], F2[:, 512:512 + w],
                gam[:, i:i + 1])
            p.v("tensor_scalar", F2[:, 512:512 + w], F2[:, 512:512 + w],
                bet[:, i:i + 1], None, ALU.add)
            if i == 0:
                p.dma(R1[0:1, 0:w], mskTd[8:9, t0:t0 + w])
                p.mm([(pt[:, 0:w] if False else pr[:, 0:w], ONESR[:],
                       R1[0:1, 0:w], True, True)]) if False else None
                p.mm([(pv[:, 0:w], ONESR[:], R1[0:1, 0:w], True, True)])
                p.v("tensor_copy", F1[:, 4096:4096 + w], pv[:, 0:w])
            p.dma(F1[:, 4608:4608 + w], yTd[i * 128:(i + 1) * 128, t0:t0 + w])
            p.v("tensor_tensor", F1[:, 4608:4608 + w], F1[:, 4608:4608 + w],
                F1[:, 4096:4096 + w], ALU.mult)
            p.s("activation", F1[:, 4608:4608 + w], F1[:, 4608:4608 + w],
                AF.Silu)
            p.v("tensor_tensor", F2[:, 512:512 + w], F2[:, 512:512 + w],
                F1[:, 4608:4608 + w], ALU.mult)
            p.dma(F1[:, 5120:5120 + w], x1d[i * 128:(i + 1) * 128, t0:t0 + w])
            p.v("tensor_tensor", F2[:, 512:512 + w], F2[:, 512:512 + w],
                F1[:, 5120:5120 + w], ALU.add)
            p.dma(outd[i * 128:(i + 1) * 128, t0:t0 + w], F2[:, 512:512 + w],
                  q="gpsimd")
        fm_ln(y1d, OWN, fin_out)

    return p.finish()


def _c(a):
    return np.ascontiguousarray(a, dtype=np.float32)


def _cb(a):
    return np.ascontiguousarray(np.asarray(a).astype(NBF))


def _chunk8(vec):
    return _c(vec.reshape(8, 128).T)


def host_layout(xT_seq, p):
    out = np.zeros((D, LW), np.float32)
    valid = np.zeros(LW, bool)
    for li, g in enumerate(block_order(p)):
        lo = g * 512 - 128
        a, b = max(lo, 0), min(g * 512 + 640, T)
        out[:, li * SEG + (a - lo): li * SEG + (b - lo)] = xT_seq[:, a:b]
        valid[li * SEG + (a - lo): li * SEG + (b - lo)] = True
    return out, valid


def host_masks(p):
    pm = np.zeros((4, 16, 128, 512), np.float32)
    kl = np.arange(128)[:, None]
    ql = np.arange(512)[None, :]
    order = block_order(p)
    for c in range(4):
        qg0 = (4 * c + p) * 512
        mi = 0
        for (lb, masked) in kb_blocks(c):
            for j in range(4):
                if masked:
                    kg = order[lb] * 512 + j * 128
                    pm[c, mi] = (kg + kl) <= (qg0 + ql)
                    mi += 1
        assert mi == 16
    return _cb(pm.reshape(64, 128, 512).transpose(1, 0, 2))


def host_inputs(ins, x, c):
    s, p = c // 4, c % 4
    xT, valid = host_layout(np.ascontiguousarray(x[s].T), p)
    kl = np.arange(128)
    ql = np.arange(128)
    band = np.zeros((128, 384), np.float32)
    for kbi in range(3):
        rel = (kbi * 128 + kl[:, None]) - ql[None, :]
        band[:, kbi * 128:(kbi + 1) * 128] = ((rel >= 0) & (rel < 256))
    return {
        "xTp": xT,
        "km": _cb(np.broadcast_to(valid, (128, LW))),
        "vm": _c(np.ascontiguousarray(valid.reshape(NSEG * 6, 128).T)),
        "pmsk": host_masks(p),
        "band": _cb(band),
        "wqp": _c(ins["L0_attn_Wq"][:, PERM_COLS]),
        "bqp": _chunk8(ins["L0_attn_bq"][PERM_COLS]),
        "wk0": _c(ins["L0_attn_Wk"]),
        "bk0": _c(ins["L0_attn_bk"][:, None]),
        "wv0": _c(ins["L0_attn_Wv"]),
        "bv0": _c(ins["L0_attn_bv"][None, :]),
        "wo": _c(ins["L0_attn_Wo"]),
        "bo": _chunk8(ins["L0_attn_bo"]),
        "wrT": _c(ins["L0_ffn_W_r"].T),
        "wkT": _c(ins["L0_ffn_W_k"].T),
        "wvT": _c(ins["L0_ffn_W_v"].T),
        "mur": _chunk8(ins["L0_ffn_mu_r"]),
        "muk": _chunk8(ins["L0_ffn_mu_k"]),
        "wq1": _c(ins["L1_attn_w_qkv"][:, 0:D]),
        "wk1": _c(ins["L1_attn_w_qkv"][:, D:2 * D]),
        "wv1": _c(ins["L1_attn_w_qkv"][:, 2 * D:3 * D]),
        "wr": _c(np.concatenate([ins["L1_ffn_Wr"][i * 128:(i + 1) * 128]
                                 for i in range(8)], axis=1)),
        "br": _c(ins["L1_ffn_br"][None, :]),
        "wp": _c(ins["L1_attn_w_proj"]),
        "bp": _chunk8(ins["L1_attn_b_proj"]),
        "w1": _c(ins["L1_ffn_W1"]),
        "w2": _c(ins["L1_ffn_W2"]),
        "gam": _chunk8(ins["L1_attn_ln_gamma"]),
        "bet": _chunk8(ins["L1_attn_ln_beta"]),
        "ident": np.eye(128, dtype=np.float32),
    }


_PROG = [None]


def kernel(**inputs):
    ins = {k: np.asarray(v, np.float32) for k, v in inputs.items()}
    x = ins["L0_attn_x"]
    if _PROG[0] is None:
        _PROG[0] = build()
    prog = _PROG[0]
    maps = [host_inputs(ins, x, c) for c in range(8)]
    res = run_bass_kernel_spmd(prog.nc, maps, core_ids=list(range(8)))
    out = np.zeros((2, T, D), np.float32)
    for c in range(8):
        s, p = c // 4, c % 4
        o = res.results[c]["outd"]
        for cc in range(4):
            g = 4 * cc + p
            out[s, g * 512:(g + 1) * 512] = o[:, cc * 512:(cc + 1) * 512].T
    return out


# version 07 (final)
# speedup vs baseline: 1.1104x; 1.0045x over previous
import numpy as np
import ml_dtypes
import concourse.bass as bass
import concourse.mybir as mybir
from concourse.bass_utils import run_bass_kernel_spmd

F32 = mybir.dt.float32
BF16 = mybir.dt.bfloat16
AF = mybir.ActivationFunctionType
ALU = mybir.AluOpType
AX = mybir.AxisListType.X
NBF = np.dtype(ml_dtypes.bfloat16)

T, D = 8192, 1024
NSEG, SEGW, SEG = 16, 512, 768
LW = NSEG * SEG
OWN = 2048
ENGS = ("sync", "gpsimd", "vector", "scalar", "tensor")
L0PERM = [h for o in range(8) for h in (o, o + 8)]
PERM_COLS = np.concatenate([np.arange(h * 64, (h + 1) * 64) for h in L0PERM])

STAGE_LO, STAGE_HI = 0, 9
STAGES = ["ln0", "ffn0", "att0", "cmb0", "ln1", "qkv1", "att1", "proj1",
          "moe", "fin"]


def block_order(p):
    own = [4 * c + p for c in range(4)]
    rest = [g for g in range(16) if g % 4 != p]
    return own + rest


def kb_blocks(c):
    out = []
    for lb in range(c + 1):
        out.append((lb, lb == c))
    for j in range(3 * (c + 1)):
        out.append((4 + j, j >= 3 * c))
    return out


class Prog:
    def __init__(self):
        self.nc = bass.Bass("TRN2", target_bir_lowering=False)
        self.ops = []
        self._keep = []

    def din(self, name, shape, dtype=F32):
        return self.nc.dram_tensor(name, shape, dtype, kind="ExternalInput").ap()

    def dhand(self, name, shape, dtype, producer):
        kind = ("ExternalInput" if STAGES.index(producer) < STAGE_LO
                else "ExternalOutput")
        return self.nc.dram_tensor(name, shape, dtype, kind=kind).ap()

    def sb(self, name, shape, dtype):
        return self.nc.alloc_sbuf_tensor(name, shape, dtype)

    def psum(self, name, shape, dtype=F32):
        cm = self.nc.psum_tensor(name, shape, dtype)
        t = cm.__enter__()
        self._keep.append(cm)
        return t

    def _e(self, eng, meth, args, kw):
        self.ops.append(
            (eng, lambda e, m=meth, a=args, k=kw: getattr(e, m)(*a, **k),
             1, False))

    def v(self, meth, *args, **kw):
        self._e("vector", meth, args, kw)

    def s(self, meth, *args, **kw):
        self._e("scalar", meth, args, kw)

    def t(self, meth, *args, **kw):
        self._e("tensor", meth, args, kw)

    def dma(self, out, in_, q="sync"):
        self.ops.append(
            (q, lambda e, o=out, i=in_: e.dma_start(out=o, in_=i), 16, False))

    def dma_group(self, pairs, q="sync"):
        def fn(e, g, pairs=tuple(pairs)):
            for (o, i_) in pairs:
                e.dma_start(out=o, in_=i_).then_inc(g, 16)
        self.ops.append((q, fn, 16 * len(pairs), True))

    def vgroup(self, items):
        def fn(e, items=tuple(items)):
            for (m, a, k) in items:
                ins = getattr(e, m)(*a, **k)
            return ins
        self.ops.append(("vector", fn, 1, False))

    def sgroup(self, items):
        def fn(e, items=tuple(items)):
            for (m, a, k) in items:
                ins = getattr(e, m)(*a, **k)
            return ins
        self.ops.append(("scalar", fn, 1, False))

    def mm(self, items):
        def fn(e, items=tuple(items)):
            for (o, l, r, st, sp) in items:
                ins = e.matmul(o, l, r, start=st, stop=sp)
            return ins
        self.ops.append(("tensor", fn, 1, False))

    def finish(self):
        cm = self.nc.semaphore("g")
        g = cm.__enter__()
        self._keep.append(cm)
        sched = []
        tot = 0
        for eng, fn, inc, selfinc in self.ops:
            sched.append((eng, fn, inc, selfinc, tot))
            tot += inc
        with self.nc.Block() as blk:
            for engname in ENGS:
                @getattr(blk, engname)
                def _(e, engname=engname):
                    for eng, fn, inc, selfinc, before in sched:
                        if eng != engname:
                            continue
                        if before:
                            e.wait_ge(g, before)
                        if selfinc:
                            fn(e, g)
                        else:
                            fn(e).then_inc(g, inc)
                    e.wait_ge(g, tot)
        return self


def build():
    p = Prog()
    lo, hi = STAGE_LO, STAGE_HI

    def on(st):
        return lo <= STAGES.index(st) <= hi

    xTp = p.din("xTp", [D, LW])
    wqp_d = p.din("wqp", [D, D])
    bqp_d = p.din("bqp", [128, 8])
    wk_d = p.din("wk0", [D, 128])
    bk_d = p.din("bk0", [128, 1])
    wv_d = p.din("wv0", [D, 128])
    bv_d = p.din("bv0", [1, 128])
    band_d = p.din("band", [128, 384], BF16)
    km_d = p.din("km", [128, LW], BF16)
    vm_d = p.din("vm", [128, NSEG * 6])
    wo_d = p.din("wo", [D, D])
    bo_d = p.din("bo", [128, 8])
    wrT_d = p.din("wrT", [D, D])
    wkT_d = p.din("wkT", [D, D])
    wvT_d = p.din("wvT", [D, D])
    mur_d = p.din("mur", [128, 8])
    muk_d = p.din("muk", [128, 8])
    wq1_d = p.din("wq1", [D, D])
    wk1_d = p.din("wk1", [D, D])
    wv1_d = p.din("wv1", [D, D])
    wr_d = p.din("wr", [128, 64])
    br_d = p.din("br", [1, 8])
    wp_d = p.din("wp", [D, D])
    bp_d = p.din("bp", [128, 8])
    w1_d = p.din("w1", [8, D, 4096])
    w2_d = p.din("w2", [8, 4096, D])
    gam_d = p.din("gam", [128, 8])
    bet_d = p.din("bet", [128, 8])
    ident_d = p.din("ident", [128, 128])
    pmsk_d = p.din("pmsk", [128, 64, 512], BF16)
    xn0d = p.dhand("xn0d", [D, LW], BF16, "ln0")
    g0d = p.dhand("g0d", [D, T], BF16, "ffn0")
    a0d = p.dhand("a0d", [D, T], BF16, "att0")
    x1d = p.dhand("x1d", [D, T], F32, "cmb0")
    xn1d = p.dhand("xn1d", [D, T], F32, "ln1")
    xn1bd = p.dhand("xn1bd", [D, T], BF16, "ln1")
    k1d = p.dhand("k1d", [D, T], BF16, "qkv1")
    vbd = p.dhand("vbd", [128, 64, D], BF16, "qkv1")
    qTd = p.dhand("qTd", [D, OWN], BF16, "qkv1")
    mskTd = p.dhand("mskTd", [16, OWN], F32, "qkv1")
    a1d = p.dhand("a1d", [D, OWN], BF16, "att1")
    y1d = p.dhand("y1d", [D, OWN], F32, "proj1")
    yTd = p.dhand("yTd", [D, OWN], F32, "moe")
    outd = p.nc.dram_tensor("outd", [D, OWN], F32, kind="ExternalOutput").ap()

    BA = p.sb("BA", [128, 16384], BF16)
    BB = p.sb("BB", [128, 16384], BF16)
    BC = p.sb("BC", [128, 16384], BF16)
    BD = p.sb("BD", [128, 8192], BF16)
    F1 = p.sb("F1", [128, 8192], F32)
    F2 = p.sb("F2", [128, 4096], F32)
    SM = p.sb("SM", [128, 2048], F32)
    RW = p.sb("RW", [16, 2048], F32)
    R1 = p.sb("R1", [1, 1024], F32)
    ST = p.sb("ST", [1, 2048], F32)
    SB2 = p.sb("SB2", [128, 2048], BF16)
    ID = p.sb("ID", [128, 128], F32)
    ONES = p.sb("ONES", [128, 1], F32)
    ONESR = p.sb("ONESR", [1, 128], F32)
    pa = p.psum("pa", [128, 512], F32)
    pv = p.psum("pv", [128, 512], F32)
    sps = p.psum("sps", [128, 512], F32)
    ops_ = p.psum("ops", [128, 260], F32)
    ops2 = p.psum("ops2", [128, 260], F32)
    pt = p.psum("pt", [128, 128], F32)
    pr = p.psum("pr", [128, 128], F32)
    prow = p.psum("prow", [1, 512], F32)

    p.v("memset", ONES[:], 1.0)
    p.v("memset", ONESR[:], 1.0)
    p.v("memset", ST[0:1, 2040:2041], 1e-5)
    p.dma(ID[:], ident_d[:, :])

    def cast_w(dst, src, cols=1024):
        p.dma_group([(F1[:, i * cols:(i + 1) * cols],
                      src[i * 128:(i + 1) * 128, 0:cols]) for i in range(8)])
        p.vgroup([("tensor_copy",
                   (dst[:, i * cols:(i + 1) * cols],
                    F1[:, i * cols:(i + 1) * cols]), {}) for i in range(8)])

    def fm_ln(src_ap, width, out_tiles):
        assert width % 512 == 0
        w = 512
        for t0 in range(0, width, 512):
            p.dma_group([(F1[:, i * w:(i + 1) * w],
                          src_ap[i * 128:(i + 1) * 128, t0:t0 + w])
                         for i in range(8)])
            p.mm([(prow[0:1, 0:w], ONES[:], F1[:, i * w:(i + 1) * w],
                   i == 0, i == 7) for i in range(8)])
            p.v("tensor_scalar_mul", ST[0:1, 0:w], prow[0:1, 0:w], 1.0 / D)
            for i in range(8):
                p.s("activation", F2[:, 0:w], F1[:, i * w:(i + 1) * w],
                    AF.Square)
                p.mm([(prow[0:1, 0:w], ONES[:], F2[:, 0:w], i == 0, i == 7)])
            p.v("tensor_scalar_mul", ST[0:1, 512:512 + w], prow[0:1, 0:w],
                1.0 / D)
            p.v("tensor_mul", ST[0:1, 1024:1024 + w], ST[0:1, 0:w],
                ST[0:1, 0:w])
            p.v("tensor_tensor", ST[0:1, 512:512 + w], ST[0:1, 512:512 + w],
                ST[0:1, 1024:1024 + w], ALU.subtract)
            p.s("activation", ST[0:1, 512:512 + w], ST[0:1, 512:512 + w],
                AF.Ln, bias=ST[0:1, 2040:2041])
            p.s("activation", ST[0:1, 512:512 + w], ST[0:1, 512:512 + w],
                AF.Exp, scale=-0.5)
            p.mm([(pv[:, 0:w], ONESR[:], ST[0:1, 0:w], True, True)])
            p.v("tensor_copy", SM[:, 1024:1024 + w], pv[:, 0:w])
            p.mm([(pv[:, 0:w], ONESR[:], ST[0:1, 512:512 + w], True, True)])
            p.v("tensor_copy", SM[:, 1536:1536 + w], pv[:, 0:w])
            for i in range(8):
                p.v("tensor_tensor", F2[:, 0:w], F1[:, i * w:(i + 1) * w],
                    SM[:, 1024:1024 + w], ALU.subtract)
                p.v("tensor_tensor", F2[:, 512:512 + w], F2[:, 0:w],
                    SM[:, 1536:1536 + w], ALU.mult)
                out_tiles(i, t0, w)

    if on("ln0"):
        def ln0_out(i, t0, w):
            p.v("tensor_copy", BD[:, 0:w], F2[:, 512:512 + w])
            p.dma(xn0d[i * 128:(i + 1) * 128, t0:t0 + w], BD[:, 0:w],
                  q="gpsimd")
        fm_ln(xTp, LW, ln0_out)

    if on("ffn0"):
        mur = SM[:, 0:8]
        muk = SM[:, 8:16]
        m1r = SM[:, 16:24]
        m1k = SM[:, 24:32]
        p.dma(mur, mur_d[:, :])
        p.dma(muk, muk_d[:, :])
        p.v("tensor_scalar", m1r, mur, -1.0, 1.0, ALU.mult, ALU.add)
        p.v("tensor_scalar", m1k, muk, -1.0, 1.0, ALU.mult, ALU.add)
        cast_w(BA[:, 0:8192], wrT_d)
        cast_w(BA[:, 8192:16384], wkT_d)
        cast_w(BB[:, 0:8192], wvT_d)
        xn = BC[:, 0:4104]
        xr = BC[:, 4104:8200]
        xk = BC[:, 8200:12296]
        rf = BD[:, 512:4608]
        for sg in range(NSEG):
            base = sg * SEG
            p.dma_group([(xn[:, i * 513:(i + 1) * 513],
                          xn0d[i * 128:(i + 1) * 128, base + 127:base + 640])
                         for i in range(8)])
            own_i = [xn[:, i * 513 + 1: i * 513 + 513] for i in range(8)]
            prev_i = [xn[:, i * 513: i * 513 + 512] for i in range(8)]
            f2_i = [F2[:, i * 512:(i + 1) * 512] for i in range(8)]
            p.vgroup([("tensor_scalar_mul",
                       (f2_i[i], prev_i[i], m1r[:, i:i + 1]), {})
                      for i in range(8)])
            p.vgroup([("scalar_tensor_tensor",
                       (xr[:, i * 512:(i + 1) * 512], own_i[i],
                        mur[:, i:i + 1], f2_i[i], ALU.mult, ALU.add), {})
                      for i in range(8)])
            p.vgroup([("tensor_scalar_mul",
                       (f2_i[i], prev_i[i], m1k[:, i:i + 1]), {})
                      for i in range(8)])
            p.vgroup([("scalar_tensor_tensor",
                       (xk[:, i * 512:(i + 1) * 512], own_i[i],
                        muk[:, i:i + 1], f2_i[i], ALU.mult, ALU.add), {})
                      for i in range(8)])
            for oc in range(0, 8, 2):
                p.mm([(pa[:, 0:512],
                       BA[:, i * 1024 + oc * 128: i * 1024 + (oc + 1) * 128],
                       xr[:, i * 512:(i + 1) * 512], i == 0, i == 7)
                      for i in range(8)] +
                     [(sps[:, 0:512],
                       BA[:, i * 1024 + (oc + 1) * 128:
                          i * 1024 + (oc + 2) * 128],
                       xr[:, i * 512:(i + 1) * 512], i == 0, i == 7)
                      for i in range(8)])
                p.sgroup([("activation",
                           (rf[:, oc * 512:(oc + 1) * 512], pa[:, 0:512],
                            AF.Sigmoid), {}),
                          ("activation",
                           (rf[:, (oc + 1) * 512:(oc + 2) * 512],
                            sps[:, 0:512], AF.Sigmoid), {})])
            for oc in range(0, 8, 2):
                p.mm([(pa[:, 0:512],
                       BA[:, 8192 + i * 1024 + oc * 128:
                          8192 + i * 1024 + (oc + 1) * 128],
                       xk[:, i * 512:(i + 1) * 512], i == 0, i == 7)
                      for i in range(8)] +
                     [(sps[:, 0:512],
                       BA[:, 8192 + i * 1024 + (oc + 1) * 128:
                          8192 + i * 1024 + (oc + 2) * 128],
                       xk[:, i * 512:(i + 1) * 512], i == 0, i == 7)
                      for i in range(8)])
                p.sgroup([("activation", (F2[:, 0:512], pa[:, 0:512],
                                          AF.Relu), {}),
                          ("activation", (F2[:, 512:1024], sps[:, 0:512],
                                          AF.Relu), {})])
                p.sgroup([("activation",
                           (xr[:, oc * 512:(oc + 1) * 512], F2[:, 0:512],
                            AF.Square), {}),
                          ("activation",
                           (xr[:, (oc + 1) * 512:(oc + 2) * 512],
                            F2[:, 512:1024], AF.Square), {})])
            for oc in range(0, 8, 2):
                p.mm([(pa[:, 0:512],
                       BB[:, i * 1024 + oc * 128: i * 1024 + (oc + 1) * 128],
                       xr[:, i * 512:(i + 1) * 512], i == 0, i == 7)
                      for i in range(8)] +
                     [(sps[:, 0:512],
                       BB[:, i * 1024 + (oc + 1) * 128:
                          i * 1024 + (oc + 2) * 128],
                       xr[:, i * 512:(i + 1) * 512], i == 0, i == 7)
                      for i in range(8)])
                p.vgroup([("tensor_mul",
                           (BD[:, 0:512], pa[:, 0:512],
                            rf[:, oc * 512:(oc + 1) * 512]), {}),
                          ("tensor_mul",
                           (BD[:, 4608:5120], sps[:, 0:512],
                            rf[:, (oc + 1) * 512:(oc + 2) * 512]), {})])
                p.dma_group([(g0d[oc * 128:(oc + 1) * 128,
                                  sg * 512:(sg + 1) * 512], BD[:, 0:512]),
                             (g0d[(oc + 1) * 128:(oc + 2) * 128,
                                  sg * 512:(sg + 1) * 512],
                              BD[:, 4608:5120])], q="gpsimd")

    if on("att0"):
        bqs = SM[:, 0:8]
        bks = SM[:, 8:9]
        bvb = SM[:, 128:256]
        vms = SM[:, 256:352]
        otm = SM[:, 1024:2048]
        bvr = R1[0:1, 0:128]
        p.dma(bqs, bqp_d[:, :])
        p.dma(bks, bk_d[:, :])
        p.dma(bvr, bv_d[:, :])
        p.dma(vms, vm_d[:, :])
        p.mm([(pa[:, 0:128], ONESR[:], bvr, True, True)])
        p.v("tensor_copy", bvb, pa[:, 0:128])
        cast_w(BA[:, 0:8192], wqp_d)
        cast_w(BB[:, 0:1024], wk_d, cols=128)
        cast_w(BB[:, 1024:2048], wv_d, cols=128)
        band = BD[:, 0:384]
        kmt = BD[:, 384:1152]
        p.dma(band[:, :], band_d[:, :])
        xh = BC[:, 0:6144]
        qT = BC[:, 6144:10240]
        kT = BC[:, 10240:11008]
        vas = BC[:, 11008:11788]
        va = vas.rearrange("p (n c) -> p n c", c=130)
        p.v("memset", va[:, :, 64:65], 1.0)
        p.v("memset", va[:, :, 129:130], 1.0)
        pp = SB2[:, 0:384]
        pm = SB2[:, 384:768]
        pp2 = SB2[:, 768:1152]
        pm2 = SB2[:, 1152:1536]
        for sg in range(NSEG):
            base = sg * SEG
            p.dma(kmt[:, :], km_d[:, base:base + SEG])
            p.dma_group([(xh[:, i * 768:(i + 1) * 768],
                          xn0d[i * 128:(i + 1) * 128, base:base + SEG])
                         for i in range(8)])
            for oc in range(8):
                p.mm([(pa[:, 0:512],
                       BA[:, i * 1024 + oc * 128: i * 1024 + (oc + 1) * 128],
                       xh[:, i * 768 + 128: i * 768 + 640], i == 0, i == 7)
                      for i in range(8)])
                p.s("activation", qT[:, oc * 512:(oc + 1) * 512], pa[:, 0:512],
                    AF.Identity, bias=bqs[:, oc:oc + 1])
            for t0, w in ((0, 512), (512, 256)):
                p.mm([(pa[:, 0:w], BB[:, i * 128:(i + 1) * 128],
                       xh[:, i * 768 + t0: i * 768 + t0 + w], i == 0, i == 7)
                      for i in range(8)])
                p.s("activation", F2[:, 0:w], pa[:, 0:w], AF.Identity,
                    bias=bks[:, 0:1])
                p.v("tensor_mul", kT[:, t0:t0 + w], F2[:, 0:w],
                    kmt[:, t0:t0 + w])
            for tt in range(6):
                p.mm([(pv[:, 0:128],
                       xh[:, i * 768 + tt * 128: i * 768 + (tt + 1) * 128],
                       BB[:, 1024 + i * 128:1024 + (i + 1) * 128],
                       i == 0, i == 7) for i in range(8)])
                p.v("tensor_tensor", F2[:, 0:128], pv[:, 0:128], bvb, ALU.add)
                p.v("tensor_scalar_mul", F2[:, 128:256], F2[:, 0:128],
                    vms[:, sg * 6 + tt: sg * 6 + tt + 1])
                p.v("tensor_copy", va[:, tt, 0:64], F2[:, 128:192])
                p.v("tensor_copy", va[:, tt, 65:129], F2[:, 192:256])
            for qb in range(4):
                for o in range(8):
                    p.mm([(sps[:, kbi * 128:(kbi + 1) * 128],
                           kT[0:64, (qb + kbi) * 128:(qb + kbi + 1) * 128],
                           qT[0:64, o * 512 + qb * 128:
                              o * 512 + (qb + 1) * 128],
                           True, True) for kbi in range(3)] +
                         [(pv[:, kbi * 128:(kbi + 1) * 128],
                           kT[64:128, (qb + kbi) * 128:(qb + kbi + 1) * 128],
                           qT[64:128, o * 512 + qb * 128:
                              o * 512 + (qb + 1) * 128],
                           True, True) for kbi in range(3)])
                    p.sgroup([("activation", (pp[:], sps[:, 0:384], AF.Exp),
                               {}),
                              ("activation", (pp2[:], pv[:, 0:384], AF.Exp),
                               {})])
                    p.vgroup([("tensor_mul", (pm[:], pp[:], band[:, :]), {}),
                              ("tensor_mul", (pm2[:], pp2[:], band[:, :]),
                               {})])
                    p.mm([(ops_[:, 0:65], pm[:, kbi * 128:(kbi + 1) * 128],
                           va[:, qb + kbi, 0:65],
                           kbi == 0, kbi == 2) for kbi in range(3)] +
                         [(pr[:, 0:65], pm2[:, kbi * 128:(kbi + 1) * 128],
                           va[:, qb + kbi, 65:130],
                           kbi == 0, kbi == 2) for kbi in range(3)])
                    p.vgroup([("reciprocal", (F2[:, 1024:1025],
                                              ops_[:, 64:65]), {}),
                              ("reciprocal", (F2[:, 1025:1026],
                                              pr[:, 64:65]), {})])
                    p.vgroup([("tensor_scalar_mul",
                               (otm[:, o * 64:(o + 1) * 64], ops_[:, 0:64],
                                F2[:, 1024:1025]), {}),
                              ("tensor_scalar_mul",
                               (otm[:, (o + 8) * 64:(o + 9) * 64],
                                pr[:, 0:64], F2[:, 1025:1026]), {})])
                for j in range(8):
                    p.t("transpose", pt[:], otm[:, j * 128:(j + 1) * 128],
                        ID[:])
                    p.v("tensor_copy",
                        BD[:, 1152 + j * 128:1152 + (j + 1) * 128], pt[:])
                p.dma(a0d.rearrange("(c p) t -> p c t", p=128)[
                    :, :, sg * 512 + qb * 128: sg * 512 + (qb + 1) * 128],
                    BD[:, 1152:2176].rearrange("p (c t) -> p c t", t=128),
                    q="gpsimd")

    if on("cmb0"):
        bos = SM[:, 0:8]
        p.dma(bos, bo_d[:, :])
        cast_w(BA[:, 0:8192], wo_d)
        for tt in range(16):
            p.dma_group([(BC[:, i * 512:(i + 1) * 512],
                          a0d[i * 128:(i + 1) * 128, tt * 512:(tt + 1) * 512])
                         for i in range(8)])
            for oc in range(8):
                p.mm([(pa[:, 0:512],
                       BA[:, i * 1024 + oc * 128: i * 1024 + (oc + 1) * 128],
                       BC[:, i * 512:(i + 1) * 512], i == 0, i == 7)
                      for i in range(8)])
                p.s("activation", F2[:, 0:512], pa[:, 0:512], AF.Identity,
                    bias=bos[:, oc:oc + 1])
                p.dma(BD[:, 0:512], g0d[oc * 128:(oc + 1) * 128,
                                        tt * 512:(tt + 1) * 512])
                p.s("activation", F2[:, 512:1024], BD[:, 0:512], AF.Silu)
                p.dma(F1[:, 0:512], xTp[oc * 128:(oc + 1) * 128,
                                        tt * 768 + 128: tt * 768 + 640])
                p.v("tensor_mul", F2[:, 1024:1536], F2[:, 0:512],
                    F2[:, 512:1024])
                p.v("tensor_tensor", F2[:, 1024:1536], F2[:, 1024:1536],
                    F1[:, 0:512], ALU.add)
                p.dma(x1d[oc * 128:(oc + 1) * 128, tt * 512:(tt + 1) * 512],
                      F2[:, 1024:1536], q="gpsimd")

    if on("ln1"):
        def ln1_out(i, t0, w):
            p.dma(xn1d[i * 128:(i + 1) * 128, t0:t0 + w], F2[:, 512:512 + w],
                  q="gpsimd")
            p.v("tensor_copy", BD[:, 0:w], F2[:, 512:512 + w])
            p.dma(xn1bd[i * 128:(i + 1) * 128, t0:t0 + w], BD[:, 0:w],
                  q="gpsimd")
        fm_ln(x1d, T, ln1_out)

    if on("qkv1"):
        cast_w(BA[:, 0:8192], wk1_d)
        cast_w(BB[:, 0:8192], wv1_d)
        for tt in range(16):
            p.dma_group([(BC[:, i * 512:(i + 1) * 512],
                          xn1bd[i * 128:(i + 1) * 128,
                                tt * 512:(tt + 1) * 512])
                         for i in range(8)])
            for oc in range(8):
                p.mm([(pa[:, 0:512],
                       BA[:, i * 1024 + oc * 128: i * 1024 + (oc + 1) * 128],
                       BC[:, i * 512:(i + 1) * 512], i == 0, i == 7)
                      for i in range(8)])
                p.v("tensor_copy", BD[:, 0:512], pa[:, 0:512])
                p.dma(k1d[oc * 128:(oc + 1) * 128, tt * 512:(tt + 1) * 512],
                      BD[:, 0:512], q="gpsimd")
            for j in range(4):
                for hf in range(2):
                    p.mm([(pv[:, 0:512],
                           BC[:, i * 512 + j * 128: i * 512 + (j + 1) * 128],
                           BB[:, i * 1024 + hf * 512:
                              i * 1024 + hf * 512 + 512],
                           i == 0, i == 7) for i in range(8)])
                    p.v("tensor_copy", BD[:, hf * 512:(hf + 1) * 512],
                        pv[:, 0:512])
                p.dma(vbd[:, tt * 4 + j, :], BD[:, 0:1024], q="gpsimd")
        cast_w(BA[:, 0:8192], wq1_d)
        for tt in range(4):
            p.dma_group([(BC[:, i * 512:(i + 1) * 512],
                          xn1bd[i * 128:(i + 1) * 128,
                                tt * 512:(tt + 1) * 512])
                         for i in range(8)])
            for oc in range(8):
                p.mm([(pa[:, 0:512],
                       BA[:, i * 1024 + oc * 128: i * 1024 + (oc + 1) * 128],
                       BC[:, i * 512:(i + 1) * 512], i == 0, i == 7)
                      for i in range(8)])
                p.v("tensor_copy", BD[:, 0:512], pa[:, 0:512])
                p.dma(qTd[oc * 128:(oc + 1) * 128, tt * 512:(tt + 1) * 512],
                      BD[:, 0:512], q="gpsimd")
        wrs = SM[:, 0:64]
        brb = SM[:, 64:72]
        rt = SM[:, 72:80]
        mps = SM[:, 80:81]
        pk = SM[:, 96:112]
        brs = R1[0:1, 128:136]
        p.dma(wrs, wr_d[:, :])
        p.dma(brs, br_d[:, :])
        p.mm([(pa[:, 0:8], ONESR[:], brs, True, True)])
        p.v("tensor_copy", brb, pa[:, 0:8])
        for tt in range(16):
            p.dma_group([(F1[:, i * 128:(i + 1) * 128],
                          xn1d[i * 128:(i + 1) * 128,
                               tt * 128:(tt + 1) * 128])
                         for i in range(8)])
            p.mm([(pr[:, 0:8], F1[:, i * 128:(i + 1) * 128],
                   wrs[:, i * 8:(i + 1) * 8], i == 0, i == 7)
                  for i in range(8)])
            p.v("tensor_tensor", rt, pr[:, 0:8], brb, ALU.add)
            p.s("activation", rt, rt, AF.Sigmoid)
            p.v("reduce_max", mps, rt, AX)
            p.v("tensor_scalar", pk[:, 0:8], rt, mps[:, 0:1], None, ALU.is_ge)
            p.v("tensor_copy", pk[:, 8:9], mps)
            p.v("memset", pk[:, 9:16], 0.0)
            p.t("transpose", pt[0:16, :], pk, ID[:])
            p.v("tensor_copy", RW[:, tt * 128:(tt + 1) * 128], pt[0:16, :])
        p.dma(mskTd[:, :], RW[:, :], q="gpsimd")

    if on("att1"):
        p.dma_group([(BA[:, i * OWN:(i + 1) * OWN],
                      qTd[i * 128:(i + 1) * 128, :]) for i in range(8)])
        kc = BB[:, 0:8192]
        mk = BB[:, 8192:16384]
        va = BC[:, 0:4160].rearrange("p (n c) -> p n c", c=65)
        va2 = BC[:, 4160:8320].rearrange("p (n c) -> p n c", c=65)
        p.v("memset", va[:, :, 64:65], 1.0)
        p.v("memset", va2[:, :, 64:65], 1.0)
        pp = SB2[:, 0:512]
        pm = SB2[:, 512:1024]
        pp2 = SB2[:, 1024:1536]
        pm2 = SB2[:, 1536:2048]
        for ch in range(8):
            p.dma(kc[:], k1d[ch * 128:(ch + 1) * 128, :])
            h0, h1 = ch * 2, ch * 2 + 1
            for c in range(4):
                blocks = kb_blocks(c)
                p.dma(mk.rearrange("p (n q) -> p n q", q=512),
                      pmsk_d[:, c * 16:(c + 1) * 16, :])
                p.dma_group(
                    [(va[:, bi * 4:(bi + 1) * 4, 0:64],
                      vbd[:, lb * 4:(lb + 1) * 4, h0 * 64:(h0 + 1) * 64])
                     for bi, (lb, _) in enumerate(blocks)] +
                    [(va2[:, bi * 4:(bi + 1) * 4, 0:64],
                      vbd[:, lb * 4:(lb + 1) * 4, h1 * 64:(h1 + 1) * 64])
                     for bi, (lb, _) in enumerate(blocks)])
                nkb = 4 * len(blocks)
                mi = 0
                kbi = 0
                qcols = slice(ch * OWN + c * 512, ch * OWN + (c + 1) * 512)
                for (lb, masked) in blocks:
                    for j in range(4):
                        kcol = lb * 512 + j * 128
                        p.mm([(sps[:, 0:512], kc[0:64, kcol:kcol + 128],
                               BA[0:64, qcols], True, True),
                              (pv[:, 0:512], kc[64:128, kcol:kcol + 128],
                               BA[64:128, qcols], True, True)])
                        if masked:
                            p.sgroup([("activation",
                                       (pp[:], sps[:, 0:512], AF.Exp),
                                       {"scale": 0.125}),
                                      ("activation",
                                       (pp2[:], pv[:, 0:512], AF.Exp),
                                       {"scale": 0.125})])
                            p.vgroup([("tensor_mul",
                                       (pm[:], pp[:],
                                        mk[:, mi * 512:(mi + 1) * 512]), {}),
                                      ("tensor_mul",
                                       (pm2[:], pp2[:],
                                        mk[:, mi * 512:(mi + 1) * 512]), {})])
                            mi += 1
                        else:
                            p.sgroup([("activation",
                                       (pm[:], sps[:, 0:512], AF.Exp),
                                       {"scale": 0.125}),
                                      ("activation",
                                       (pm2[:], pv[:, 0:512], AF.Exp),
                                       {"scale": 0.125})])
                        p.mm([(ops_[:, qq * 65:(qq + 1) * 65],
                               pm[:, qq * 128:(qq + 1) * 128],
                               va[:, kbi, 0:65],
                               kbi == 0 and qq == 0, kbi == nkb - 1)
                              for qq in range(4)] +
                             [(ops2[:, qq * 65:(qq + 1) * 65],
                               pm2[:, qq * 128:(qq + 1) * 128],
                               va2[:, kbi, 0:65],
                               kbi == 0 and qq == 0, kbi == nkb - 1)
                              for qq in range(4)])
                        kbi += 1
                for h2, opsx in ((0, ops_), (1, ops2)):
                    h = ch * 2 + h2
                    for qq in range(4):
                        p.v("reciprocal", F2[:, 0:1],
                            opsx[:, qq * 65 + 64:qq * 65 + 65])
                        p.v("tensor_scalar_mul", F2[:, 128:192],
                            opsx[:, qq * 65:qq * 65 + 64], F2[:, 0:1])
                        p.t("transpose", pt[:], F2[:, 128:256], ID[:])
                        p.v("tensor_copy", BD[0:64, 0:128], pt[0:64, :])
                        p.dma(a1d[h * 64:(h + 1) * 64,
                                  c * 512 + qq * 128:
                                  c * 512 + (qq + 1) * 128],
                              BD[0:64, 0:128], q="gpsimd")

    if on("proj1"):
        bps = SM[:, 0:8]
        p.dma(bps, bp_d[:, :])
        cast_w(BA[:, 0:8192], wp_d)
        for tt in range(4):
            p.dma_group([(BC[:, i * 512:(i + 1) * 512],
                          a1d[i * 128:(i + 1) * 128, tt * 512:(tt + 1) * 512])
                         for i in range(8)])
            for oc in range(8):
                p.mm([(pa[:, 0:512],
                       BA[:, i * 1024 + oc * 128: i * 1024 + (oc + 1) * 128],
                       BC[:, i * 512:(i + 1) * 512], i == 0, i == 7)
                      for i in range(8)])
                p.s("activation", F2[:, 0:512], pa[:, 0:512], AF.Identity,
                    bias=bps[:, oc:oc + 1])
                p.dma(F1[:, 0:512], xn1d[oc * 128:(oc + 1) * 128,
                                         tt * 512:(tt + 1) * 512])
                p.v("tensor_tensor", F2[:, 512:1024], F2[:, 0:512],
                    F1[:, 0:512], ALU.add)
                p.dma(y1d[oc * 128:(oc + 1) * 128, tt * 512:(tt + 1) * 512],
                      F2[:, 512:1024], q="gpsimd")

    if on("moe"):
        for s in range(2):
            p.dma_group([(BC[:, i * 1024:(i + 1) * 1024],
                          xn1bd[i * 128:(i + 1) * 128,
                                s * 1024:(s + 1) * 1024])
                         for i in range(8)])
            p.v("memset", BD[:, 0:8192], 0.0)
            for e in range(8):
                p.dma(R1[0:1, 0:1024],
                      mskTd[e:e + 1, s * 1024:(s + 1) * 1024])
                for half in range(2):
                    p.mm([(pa[:, 0:512], ONESR[:],
                           R1[0:1, half * 512:(half + 1) * 512], True, True)])
                    p.v("tensor_copy", SM[:, half * 512:(half + 1) * 512],
                        pa[:, 0:512])
                for fh in range(2):
                    for q4 in range(2):
                        p.dma_group([(F1[:, i * 2048:(i + 1) * 2048],
                                      w1_d[e, (q4 * 4 + i) * 128:
                                           (q4 * 4 + i + 1) * 128,
                                           fh * 2048:(fh + 1) * 2048])
                                     for i in range(4)])
                        p.vgroup([("tensor_copy",
                                   (BA[:, (q4 * 4 + i) * 2048:
                                       (q4 * 4 + i + 1) * 2048],
                                    F1[:, i * 2048:(i + 1) * 2048]), {})
                                  for i in range(4)])
                    for q8 in range(2):
                        p.dma_group([(F1[:, i * 1024:(i + 1) * 1024],
                                      w2_d[e, fh * 2048 + (q8 * 8 + i) * 128:
                                           fh * 2048 + (q8 * 8 + i + 1) * 128,
                                           :])
                                     for i in range(8)])
                        p.vgroup([("tensor_copy",
                                   (BB[:, (q8 * 8 + i) * 1024:
                                       (q8 * 8 + i + 1) * 1024],
                                    F1[:, i * 1024:(i + 1) * 1024]), {})
                                  for i in range(8)])
                    for tt in range(2):
                        for of in range(0, 16, 2):
                            p.mm([(pa[:, 0:512],
                                   BA[:, i * 2048 + of * 128:
                                      i * 2048 + (of + 1) * 128],
                                   BC[:, i * 1024 + tt * 512:
                                      i * 1024 + tt * 512 + 512],
                                   i == 0, i == 7) for i in range(8)] +
                                 [(sps[:, 0:512],
                                   BA[:, i * 2048 + (of + 1) * 128:
                                      i * 2048 + (of + 2) * 128],
                                   BC[:, i * 1024 + tt * 512:
                                      i * 1024 + tt * 512 + 512],
                                   i == 0, i == 7) for i in range(8)])
                            p.sgroup([("activation",
                                       (BC[:, 8192 + of * 512:
                                           8192 + (of + 1) * 512],
                                        pa[:, 0:512], AF.Gelu), {}),
                                      ("activation",
                                       (BC[:, 8192 + (of + 1) * 512:
                                           8192 + (of + 2) * 512],
                                        sps[:, 0:512], AF.Gelu), {})])
                        for oc in range(0, 8, 2):
                            p.mm([(pa[:, 0:512],
                                   BB[:, hc * 1024 + oc * 128:
                                      hc * 1024 + (oc + 1) * 128],
                                   BC[:, 8192 + hc * 512:
                                      8192 + (hc + 1) * 512],
                                   hc == 0, hc == 15) for hc in range(16)] +
                                 [(sps[:, 0:512],
                                   BB[:, hc * 1024 + (oc + 1) * 128:
                                      hc * 1024 + (oc + 2) * 128],
                                   BC[:, 8192 + hc * 512:
                                      8192 + (hc + 1) * 512],
                                   hc == 0, hc == 15) for hc in range(16)])
                            p.vgroup([("tensor_mul",
                                       (F2[:, 0:512], pa[:, 0:512],
                                        SM[:, tt * 512:(tt + 1) * 512]), {}),
                                      ("tensor_mul",
                                       (F2[:, 512:1024], sps[:, 0:512],
                                        SM[:, tt * 512:(tt + 1) * 512]), {})])
                            a0 = BD[:, oc * 1024 + tt * 512:
                                    oc * 1024 + tt * 512 + 512]
                            a1 = BD[:, (oc + 1) * 1024 + tt * 512:
                                    (oc + 1) * 1024 + tt * 512 + 512]
                            p.vgroup([("tensor_tensor",
                                       (a0, a0, F2[:, 0:512], ALU.add), {}),
                                      ("tensor_tensor",
                                       (a1, a1, F2[:, 512:1024], ALU.add),
                                       {})])
            for oc in range(8):
                p.v("tensor_copy", F1[:, 0:1024],
                    BD[:, oc * 1024:(oc + 1) * 1024])
                p.dma(yTd[oc * 128:(oc + 1) * 128, s * 1024:(s + 1) * 1024],
                      F1[:, 0:1024], q="gpsimd")

    if on("fin"):
        gam = SM[:, 0:8]
        bet = SM[:, 8:16]
        p.dma(gam, gam_d[:, :])
        p.dma(bet, bet_d[:, :])

        def fin_out(i, t0, w):
            p.v("tensor_scalar_mul", F2[:, 512:512 + w], F2[:, 512:512 + w],
                gam[:, i:i + 1])
            p.v("tensor_scalar", F2[:, 512:512 + w], F2[:, 512:512 + w],
                bet[:, i:i + 1], None, ALU.add)
            if i == 0:
                p.dma(R1[0:1, 0:w], mskTd[8:9, t0:t0 + w])
                p.mm([(pt[:, 0:w] if False else pr[:, 0:w], ONESR[:],
                       R1[0:1, 0:w], True, True)]) if False else None
                p.mm([(pv[:, 0:w], ONESR[:], R1[0:1, 0:w], True, True)])
                p.v("tensor_copy", F1[:, 4096:4096 + w], pv[:, 0:w])
            p.dma(F1[:, 4608:4608 + w], yTd[i * 128:(i + 1) * 128, t0:t0 + w])
            p.v("tensor_tensor", F1[:, 4608:4608 + w], F1[:, 4608:4608 + w],
                F1[:, 4096:4096 + w], ALU.mult)
            p.s("activation", F1[:, 4608:4608 + w], F1[:, 4608:4608 + w],
                AF.Silu)
            p.v("tensor_tensor", F2[:, 512:512 + w], F2[:, 512:512 + w],
                F1[:, 4608:4608 + w], ALU.mult)
            p.dma(F1[:, 5120:5120 + w], x1d[i * 128:(i + 1) * 128, t0:t0 + w])
            p.v("tensor_tensor", F2[:, 512:512 + w], F2[:, 512:512 + w],
                F1[:, 5120:5120 + w], ALU.add)
            p.dma(outd[i * 128:(i + 1) * 128, t0:t0 + w], F2[:, 512:512 + w],
                  q="gpsimd")
        fm_ln(y1d, OWN, fin_out)

    return p.finish()


def _c(a):
    return np.ascontiguousarray(a, dtype=np.float32)


def _cb(a):
    return np.ascontiguousarray(np.asarray(a).astype(NBF))


def _chunk8(vec):
    return _c(vec.reshape(8, 128).T)


def host_layout(xT_seq, p):
    out = np.zeros((D, LW), np.float32)
    valid = np.zeros(LW, bool)
    for li, g in enumerate(block_order(p)):
        lo = g * 512 - 128
        a, b = max(lo, 0), min(g * 512 + 640, T)
        out[:, li * SEG + (a - lo): li * SEG + (b - lo)] = xT_seq[:, a:b]
        valid[li * SEG + (a - lo): li * SEG + (b - lo)] = True
    return out, valid


def host_masks(p):
    pm = np.zeros((4, 16, 128, 512), np.float32)
    kl = np.arange(128)[:, None]
    ql = np.arange(512)[None, :]
    order = block_order(p)
    for c in range(4):
        qg0 = (4 * c + p) * 512
        mi = 0
        for (lb, masked) in kb_blocks(c):
            for j in range(4):
                if masked:
                    kg = order[lb] * 512 + j * 128
                    pm[c, mi] = (kg + kl) <= (qg0 + ql)
                    mi += 1
        assert mi == 16
    return _cb(pm.reshape(64, 128, 512).transpose(1, 0, 2))


def host_inputs(ins, x, c):
    s, p = c // 4, c % 4
    xT, valid = host_layout(np.ascontiguousarray(x[s].T), p)
    kl = np.arange(128)
    ql = np.arange(128)
    band = np.zeros((128, 384), np.float32)
    for kbi in range(3):
        rel = (kbi * 128 + kl[:, None]) - ql[None, :]
        band[:, kbi * 128:(kbi + 1) * 128] = ((rel >= 0) & (rel < 256))
    return {
        "xTp": xT,
        "km": _cb(np.broadcast_to(valid, (128, LW))),
        "vm": _c(np.ascontiguousarray(valid.reshape(NSEG * 6, 128).T)),
        "pmsk": host_masks(p),
        "band": _cb(band),
        "wqp": _c(ins["L0_attn_Wq"][:, PERM_COLS]),
        "bqp": _chunk8(ins["L0_attn_bq"][PERM_COLS]),
        "wk0": _c(ins["L0_attn_Wk"]),
        "bk0": _c(ins["L0_attn_bk"][:, None]),
        "wv0": _c(ins["L0_attn_Wv"]),
        "bv0": _c(ins["L0_attn_bv"][None, :]),
        "wo": _c(ins["L0_attn_Wo"]),
        "bo": _chunk8(ins["L0_attn_bo"]),
        "wrT": _c(ins["L0_ffn_W_r"].T),
        "wkT": _c(ins["L0_ffn_W_k"].T),
        "wvT": _c(ins["L0_ffn_W_v"].T),
        "mur": _chunk8(ins["L0_ffn_mu_r"]),
        "muk": _chunk8(ins["L0_ffn_mu_k"]),
        "wq1": _c(ins["L1_attn_w_qkv"][:, 0:D]),
        "wk1": _c(ins["L1_attn_w_qkv"][:, D:2 * D]),
        "wv1": _c(ins["L1_attn_w_qkv"][:, 2 * D:3 * D]),
        "wr": _c(np.concatenate([ins["L1_ffn_Wr"][i * 128:(i + 1) * 128]
                                 for i in range(8)], axis=1)),
        "br": _c(ins["L1_ffn_br"][None, :]),
        "wp": _c(ins["L1_attn_w_proj"]),
        "bp": _chunk8(ins["L1_attn_b_proj"]),
        "w1": _c(ins["L1_ffn_W1"]),
        "w2": _c(ins["L1_ffn_W2"]),
        "gam": _chunk8(ins["L1_attn_ln_gamma"]),
        "bet": _chunk8(ins["L1_attn_ln_beta"]),
        "ident": np.eye(128, dtype=np.float32),
    }


_PROG = [None]


def kernel(**inputs):
    ins = {k: np.asarray(v, np.float32) for k, v in inputs.items()}
    x = ins["L0_attn_x"]
    if _PROG[0] is None:
        _PROG[0] = build()
    prog = _PROG[0]
    maps = [host_inputs(ins, x, c) for c in range(8)]
    res = run_bass_kernel_spmd(prog.nc, maps, core_ids=list(range(8)))
    out = np.zeros((2, T, D), np.float32)
    for c in range(8):
        s, p = c // 4, c % 4
        o = res.results[c]["outd"]
        for cc in range(4):
            g = 4 * cc + p
            out[s, g * 512:(g + 1) * 512] = o[:, cc * 512:(cc + 1) * 512].T
    return out
